# version 07
# baseline (speedup 1.0000x reference)
import math
import numpy as np
import ml_dtypes
import concourse.bass as bass
import concourse.mybir as mybir
from concourse.bass_utils import run_bass_kernel_spmd

F32 = mybir.dt.float32
BF16 = mybir.dt.bfloat16
U8 = mybir.dt.uint8
AF = mybir.ActivationFunctionType
ALU = mybir.AluOpType
AX = mybir.AxisListType
BF = ml_dtypes.bfloat16

B_TOT, S, D = 16, 2048, 1024
B, NCORES = 2, 8
H_RET, KD, VD_H, VDIM = 8, 128, 256, 2048
DH, INNER, WINDOW, HID, NHC = 64, 512, 512, 2730, 22
TT, NTT = 512, 4
EPS = 1e-5

STAGES = dict(attn=True, ffn=True)
FFN_SUB = {'fold', 'vproj', 'qk', 'qknorm', 'qkrot', 'sigb', 'attn', 'wo', 'ln', 'w1', 'w2'}
RET_SUB = {'carve', 'stmem', 'tt', 'cast', 'vproj', 'tables', 'qkproj', 'rot', 'scores', 'ktt', 'value', 'state', 'silu', 'wo'}
TAP = False


def _tap(k, name, ap, reads):
    if not TAP or name in k.taps:
        return
    shp = [int(s) for s in np.asarray(ap.shape)] if hasattr(ap, "shape") else None
    t = k.nc.dram_tensor("tap_" + name, list(ap.shape), ap.dtype, kind="ExternalOutput")
    k.taps[name] = t
    k.P.emit("DQ", lambda e: e.dma_start(t[:, :] if len(t.shape) == 2 else t[:], ap),
             reads=reads)



def _ret_tables():
    t = np.arange(S, dtype=np.float64)
    angle = np.repeat(1.0 / 10000.0 ** np.linspace(0.0, 1.0, KD // 2), 2)
    cos = np.cos(t[None, :] * angle[:, None])
    sin = np.sin(t[None, :] * angle[:, None])
    qc = np.zeros((H_RET * KD, S)); qs = np.zeros((H_RET * KD, S))
    kc = np.zeros((H_RET * KD, S)); ks = np.zeros((H_RET * KD, S))
    for h in range(H_RET):
        d = math.log1p(-(2.0 ** (-5.0 - h)))
        n = 1.0 / np.sqrt(np.cumsum(np.exp(d * t)))
        a = np.exp(d * t) * n
        b = np.exp(-d * t) / math.sqrt(KD)
        sl = slice(h * KD, (h + 1) * KD)
        qc[sl] = cos * a[None, :]; qs[sl] = sin * a[None, :]
        kc[sl] = cos * b[None, :]; ks[sl] = sin * b[None, :]
    return qc.astype(BF), qs.astype(BF), kc.astype(BF), ks.astype(BF)


def _ffn_tables():
    t = np.arange(S, dtype=np.float64)
    hi = np.arange(0, DH, 2, dtype=np.float64)
    inv = 1.0 / 10000.0 ** (hi / DH)
    fr = t[:, None] * inv[None, :]
    base = (hi + 0.4 * DH) / (1.4 * DH)
    power = (t - S // 2) / float(WINDOW // 2)
    sc = base[None, :] ** power[:, None]

    def dup2(x):
        y = np.concatenate([x, x], axis=1).T
        return np.concatenate([y, y], axis=0)

    return (dup2(np.cos(fr) * sc).astype(BF), dup2(np.sin(fr) * sc).astype(BF),
            dup2(np.cos(fr) / sc).astype(BF), dup2(np.sin(fr) / sc).astype(BF))


def _consts():
    i = np.arange(128)
    tri_up = (i[None, :] >= i[:, None]).astype(BF)
    tri_lo = (i[None, :] <= i[:, None]).astype(BF)
    p_ret = np.zeros((128, 128), dtype=BF)
    for d in range(0, 128, 2):
        p_ret[d + 1, d] = -1.0; p_ret[d, d + 1] = 1.0
    p_ffn = np.zeros((128, 128), dtype=BF)
    for b0 in (0, 64):
        for d in range(32):
            p_ffn[b0 + d + 32, b0 + d] = -1.0; p_ffn[b0 + d, b0 + d + 32] = 1.0
    sh64 = np.zeros((64, 128), dtype=BF)
    for k in range(64):
        sh64[k, 64 + k] = 1.0
    ind = np.zeros((8, 128, 16), dtype=BF)
    indT = np.zeros((8, 16, 128), dtype=np.float32)
    for fo in range(8):
        for p in range(128):
            slot = (2 * fo + p // 64) if fo < 4 else (8 + 2 * (fo - 4) + p // 64)
            ind[fo, p, slot] = 1.0
            indT[fo, slot, p] = 8.0 if fo < 4 else 1.0
    return dict(
        m_up=tri_up, m_lo=tri_lo, p_ret=p_ret, p_ffn=p_ffn, sh64=sh64,
        id_f32=np.eye(128, dtype=np.float32), id_bf=np.eye(128, dtype=BF),
        ones64f=np.ones((1, 64), dtype=np.float32),
        ones128f=np.ones((1, 128), dtype=np.float32),
        ones128b=np.ones((128, 1), dtype=BF),
        ind=ind, indT32=indT,
    )


class Prog:

    def __init__(self, nc, ctx):
        self.nc = nc
        self.ctx = ctx
        self.eng = {"T": nc.tensor, "V": nc.vector, "S": nc.scalar,
                    "G": nc.gpsimd, "DQ": nc.sync, "GQ": nc.gpsimd}
        self.sem = {k: ctx.enter_context(nc.semaphore(f"sem_{k}"))
                    for k in ("T", "V", "S", "G")}
        self.step = {"T": 1, "V": 1, "S": 1, "G": 1}
        self.cnt = {"T": 0, "V": 0, "S": 0, "G": 0}
        self.waited = {}
        self.bufs = {}
        self.psum = set()
        self.bufsem = {}
        self.nsem = 0
        self.dram_stores = {}

    def _ent(self, b):
        return self.bufs.setdefault(id(b), {"w": None, "r": {}})

    def seed(self, b, readers):
        self.bufs[id(b)] = {"w": None, "r": dict(readers)}

    def _get_bufsem(self, b):
        key = self.bufsem.get(id(b))
        if key is None:
            key = f"b{self.nsem}"
            self.nsem += 1
            self.sem[key] = self.ctx.enter_context(self.nc.semaphore(f"sem_{key}"))
            self.step[key] = 16
            self.cnt[key] = 0
            self.bufsem[id(b)] = key
        return key

    def emit(self, eng, fn, reads=(), writes=(), dma_buf=None, waw_free=False,
             dram_store=False):
        is_dma = eng in ("DQ", "GQ")
        need = {}

        def add(src, cnt, raw):
            old = need.get(src)
            if old is None or cnt > old[0]:
                need[src] = (cnt, raw or (old[1] if old else False))
            elif raw and not old[1]:
                need[src] = (old[0], True)

        for b in reads:
            e = self._ent(b)
            if e["w"] is not None:
                add(e["w"][0], e["w"][1], True)
            if id(b) in self.psum:
                for rk, rv in e["r"].items():
                    add(rk, rv, True)
        for b in writes:
            e = self._ent(b)
            if e["w"] is not None and not waw_free:
                add(e["w"][0], e["w"][1], False)
            for rk, rv in e["r"].items():
                add(rk, rv, False)
        obj = self.eng[eng]
        phys = {"DQ": "DQ", "GQ": "G"}.get(eng, eng)
        for src in sorted(need):
            cnt, raw = need[src]
            if src == "T" and phys == "T":
                continue
            if src == phys and not raw:
                continue
            k = (phys, src)
            if self.waited.get(k, 0) >= cnt:
                continue
            obj.wait_ge(self.sem[src], cnt * self.step[src])
            self.waited[k] = cnt
        ins = fn(obj)
        if isinstance(ins, (list, tuple)):
            ins = ins[-1]
        if is_dma:
            if dma_buf is None:
                dma_buf = writes[0] if writes else reads[0]
            semkey = self._get_bufsem(dma_buf)
        else:
            semkey = eng
        self.cnt[semkey] += 1
        ins.then_inc(self.sem[semkey], self.step[semkey])
        tok = (semkey, self.cnt[semkey])
        if dram_store:
            self.dram_stores[semkey] = self.cnt[semkey]
        for b in reads:
            self._ent(b)["r"][semkey] = self.cnt[semkey]
        for b in writes:
            if waw_free:
                e = self._ent(b)
                e["r"][semkey] = self.cnt[semkey]
            else:
                self.bufs[id(b)] = {"w": tok, "r": {}}
        return tok

    def fence(self):
        for eng in ("T", "V", "S", "G", "DQ"):
            obj = self.eng[eng]
            phys = {"DQ": "DQ"}.get(eng, eng)
            for src in ("T", "V", "S", "G"):
                cnt = self.cnt[src]
                if cnt == 0 or src == phys:
                    continue
                k = (phys, src)
                if self.waited.get(k, 0) >= cnt:
                    continue
                obj.wait_ge(self.sem[src], cnt)
                self.waited[k] = cnt

    def final_wait(self):
        for semkey, cnt in self.dram_stores.items():
            self.nc.sync.wait_ge(self.sem[semkey], cnt * self.step[semkey])


class _View:
    __slots__ = ("t", "bo", "p", "n", "dt", "esz")

    def __init__(self, t, bo, p, n, dt):
        self.t, self.bo, self.p, self.n, self.dt = t, bo, p, n, dt
        self.esz = {F32: 4, BF16: 2, U8: 1}[dt]

    def a(self, p0=0, p1=None, c0=0, c1=None):
        p1 = self.p if p1 is None else p1
        c1 = self.n if c1 is None else c1
        return self.t[p0:p1, self.bo + c0 * self.esz: self.bo + c1 * self.esz].bitcast(self.dt)

    def __getitem__(self, idx):
        ps, cs = idx
        p0 = ps.start or 0
        p1 = self.p if ps.stop is None else ps.stop
        c0 = cs.start or 0
        c1 = self.n if cs.stop is None else cs.stop
        return self.a(p0, p1, c0, c1)

    def ap3(self, coff, dims, p0=0, p1=None):
        p1 = self.p if p1 is None else p1
        base = self.t[p0:p1, self.bo + coff * self.esz: self.bo + (coff + 1) * self.esz].bitcast(self.dt)
        ap = [list(base.ap[0])] + [[s, c] for s, c in dims]
        return bass.AP(base.tensor, base.offset, ap)


class Pool:

    def __init__(self, nc, ctx, prog, nbytes):
        self.t = ctx.enter_context(nc.sbuf_tensor("pool", [128, nbytes], U8))
        self.P = prog
        self.live = []
        self.retired = []
        self.off = 0
        self.nbytes = nbytes

    def reset(self):
        self.retired.extend(self.live)
        self.live = []
        self.off = 0

    def take(self, p, n, dt):
        esz = {F32: 4, BF16: 2, U8: 1}[dt]
        nb = ((n * esz + 3) // 4) * 4
        assert self.off + nb <= self.nbytes, f"pool overflow at {self.off + nb}"
        s, e_ = self.off, self.off + nb
        guard = {}
        keep = []
        for v, vs, ve in self.retired:
            if vs < e_ and s < ve:
                ent = self.P._ent(v)
                if ent["w"] is not None:
                    guard[ent["w"][0]] = max(guard.get(ent["w"][0], 0), ent["w"][1])
                for rk, rv in ent["r"].items():
                    guard[rk] = max(guard.get(rk, 0), rv)
            else:
                keep.append((v, vs, ve))
        self.retired = [(v, vs, ve) for v, vs, ve in self.retired]
        v = _View(self.t, s, p, n, dt)
        self.off = e_
        self.P.seed(v, guard)
        self.live.append((v, s, e_))
        return v


class K:
    pass


def build_program():
    nc = bass.Bass("TRN2", target_bir_lowering=False)
    k = K()
    k.nc = nc
    k.taps = {}

    def din(name, shape, dt=F32):
        return nc.dram_tensor(name, list(shape), dt, kind="ExternalInput")

    k.x_in = din("x", (B, S, D))
    k.W = {n: din(n, s) for n, s in [
        ("wq", (D, D)), ("wk", (D, D)), ("wv", (D, VDIM)), ("wg", (D, VDIM)),
        ("wo", (VDIM, D)), ("wqkv", (2, D, 3 * INNER)), ("wof", (2, INNER, D)),
        ("wa", (2, D, HID)), ("wg1", (2, D, HID)), ("w2", (2, HID, D))]}
    k.qs_c = din("qs_c", (2, 128, 4))
    k.lng_c = din("lng_c", (2, 128, 8)); k.lnb_c = din("lnb_c", (2, 128, 8))
    k.b1a_c = din("b1a_c", (2, 128, NHC)); k.b1g_c = din("b1g_c", (2, 128, NHC))
    k.b2_c = din("b2_c", (2, 128, 8))
    k.t_tab = [din(f"t_tab{i}", (H_RET * KD, S), BF16) for i in range(4)]
    k.f_tab = [din(f"f_tab{i}", (128, S), BF16) for i in range(4)]
    k.CN = {n: din(n, v.shape, {2: BF16, 4: F32}[v.dtype.itemsize])
            for n, v in _consts().items()}
    k.out_d = nc.dram_tensor("out", [B, S, D], F32, kind="ExternalOutput")
    k.WB = {n: nc.dram_tensor(n + "_b", list(k.W[n].shape), BF16, kind="Internal")
            for n in k.W}
    k.WBtok = {n: object() for n in k.W}

    from contextlib import ExitStack
    with ExitStack() as ctx:
        P = Prog(nc, ctx)
        k.P = P

        def sb(name, p, n, dt=F32):
            return ctx.enter_context(nc.sbuf_tensor(name, [p, n], dt))

        k.xf = [sb(f"xf{i}", 128, TT * NTT + 4) for i in range(8)]
        k.acc = [sb(f"acc{i}", 128, S, BF16) for i in range(8)]
        k.stf = [sb(f"st{h}", 128, 257) for h in range(H_RET)]
        k.sthi = sb("sthi", 128, 257, BF16)
        k.stlo = sb("stlo", 128, 257, BF16)
        k.cast = [[sb(f"cast{r}_{i}", 128, TT + 4, BF16) for i in range(8)] for r in range(2)]
        k.wbuf = [sb(f"wbuf{i}", 128, 1024, BF16) for i in range(4)]
        k.wbig = [sb(f"wbig{i}", 128, 2048, BF16) for i in range(2)]
        for n, p, w, dt in [("c_id32", 128, 128, F32), ("c_idbf", 128, 128, BF16),
                            ("c_pret", 128, 128, BF16), ("c_pffn", 128, 128, BF16),
                            ("c_mup", 128, 128, BF16), ("c_mlo", 128, 128, BF16),
                            ("c_sh64", 64, 128, BF16),
                            ("c_o64f", 1, 64, F32), ("c_o128f", 1, 128, F32),
                            ("c_o128b", 128, 1, BF16), ("c_qs", 128, 8, F32),
                            ("c_lng", 128, 16, F32), ("c_lnb", 128, 16, F32),
                            ("c_b1a", 128, 2 * NHC, F32), ("c_b1g", 128, 2 * NHC, F32),
                            ("c_b2", 128, 16, F32), ("eps_t", 128, 16, F32)]:
            setattr(k, n, sb(n, p, w, dt))
        k.c_ind = [sb(f"c_ind{i}", 128, 16, BF16) for i in range(8)]
        k.c_indT = [sb(f"c_indT{i}", 16, 128, F32) for i in range(8)]
        k.PBS = [ctx.enter_context(nc.psum_tensor(f"PB{i}", [128, 512], F32)) for i in range(8)]
        P.psum.update(id(pb) for pb in k.PBS)
        P.psum.update(id(t) for t in k.WBtok.values())
        k.pool = Pool(nc, ctx, P, 61520)

        with nc.Block() as _blk:
            _phase0_consts(k)
            _s0_ln(k, 0)
            _phase0_cast(k, ["wq", "wk", "wv", "wg", "wo"], "DQ")
            if STAGES["ffn"]:
                _phase0_cast(k, ["wqkv", "wof", "wa", "wg1", "w2"], "GQ")
            for b in range(B):
                if b > 0:
                    _s0_ln(k, b)
                if STAGES["attn"]:
                    _retnet(k, b)
                if STAGES["ffn"]:
                    for l in range(2):
                        _ffn_layer(k, b, l)
                        _tap_xf_final(k, b, l)
                _assemble(k, b)
            P.final_wait()
    return k


def _phase0_consts(k):
    P, emit = k.P, k.P.emit
    for name, dst in [("id_f32", k.c_id32), ("id_bf", k.c_idbf), ("p_ret", k.c_pret),
                      ("p_ffn", k.c_pffn), ("m_up", k.c_mup), ("m_lo", k.c_mlo),
                      ("sh64", k.c_sh64), ("ones64f", k.c_o64f),
                      ("ones128f", k.c_o128f), ("ones128b", k.c_o128b)]:
        emit("DQ", lambda e, d=dst, s=k.CN[name]: e.dma_start(d[:, :], s[:, :]), writes=[dst])
    for i in range(8):
        emit("DQ", lambda e, i=i: e.dma_start(k.c_ind[i][:, :], k.CN["ind"][i]),
             writes=[k.c_ind[i]])
        emit("DQ", lambda e, i=i: e.dma_start(k.c_indT[i][:, :], k.CN["indT32"][i]),
             writes=[k.c_indT[i]])
    for l in range(2):
        for dst, src, w in [(k.c_qs, k.qs_c, 4), (k.c_lng, k.lng_c, 8),
                            (k.c_lnb, k.lnb_c, 8), (k.c_b1a, k.b1a_c, NHC),
                            (k.c_b1g, k.b1g_c, NHC), (k.c_b2, k.b2_c, 8)]:
            emit("DQ", lambda e, d=dst, s=src, l=l, w=w:
                 e.dma_start(d[:, l * w:(l + 1) * w], s[l]), writes=[dst])


def _phase0_cast(k, names, queue):
    P, emit = k.P, k.P.emit
    wf32 = [k.pool.take(128, 2048, F32) for _ in range(2)]
    if queue == "DQ":
        stg = k.wbig
    else:
        stg = [k.pool.take(128, 2048, BF16) for _ in range(2)]
    ci = [0]

    def cast2(src, dst, rows, cols, tok):
        for r0 in range(0, rows, 128):
            pr = min(128, rows - r0)
            for c0 in range(0, cols, 2048):
                w = min(2048, cols - c0)
                i = ci[0] % 2; ci[0] += 1
                fb, bb = wf32[i], stg[i]
                emit(queue, lambda e, s=src, r0=r0, pr=pr, c0=c0, w=w, fb=fb:
                     e.dma_start(fb.a(0, pr, 0, w), s[r0:r0 + pr, c0:c0 + w]),
                     writes=[fb])
                emit("V", lambda e, fb=fb, bb=bb, pr=pr, w=w:
                     e.tensor_copy(bb[0:pr, 0:w] if not isinstance(bb, _View) else bb.a(0, pr, 0, w),
                                   fb.a(0, pr, 0, w)),
                     reads=[fb], writes=[bb])
                emit(queue, lambda e, d=dst, r0=r0, pr=pr, c0=c0, w=w, bb=bb:
                     e.dma_start(d[r0:r0 + pr, c0:c0 + w],
                                 bb[0:pr, 0:w] if not isinstance(bb, _View) else bb.a(0, pr, 0, w)),
                     reads=[bb], writes=[tok], dma_buf=bb, waw_free=True)

    for n in names:
        shp = k.W[n].shape
        if len(shp) == 3:
            for l in range(shp[0]):
                cast2(k.W[n][l], k.WB[n][l], shp[1], shp[2], k.WBtok[n])
        else:
            cast2(k.W[n], k.WB[n], shp[0], shp[1], k.WBtok[n])


def _do_cast(k, b, tt, cr):
    emit = k.P.emit
    for fi in range(8):
        emit("V", lambda e, fi=fi, tt=tt, cr=cr: e.tensor_copy(
            k.cast[cr][fi][:, 0:TT + 1], k.xf[fi][:, tt * TT: tt * TT + TT + 1]),
            reads=[k.xf[fi]], writes=[k.cast[cr][fi]])


def _s0_ln(k, b):
    P, emit, pool, PBS = k.P, k.P.emit, k.pool, k.PBS
    ILN = 1.0 / D
    pool.reset()
    tmx = [pool.take(128, D, F32) for _ in range(2)]
    tms = pool.take(128, D, F32)
    lnc = pool.take(128, 8, F32)
    for fc in range(8):
        emit("V", lambda e, fc=fc: e.memset(k.xf[fc][:, 0:1], 0.0), writes=[k.xf[fc]])
    for tc in range(16):
        tm = tmx[tc % 2]
        emit("DQ", lambda e, tm=tm, tc=tc: e.dma_start(
            tm.a(), k.x_in[b, tc * 128:(tc + 1) * 128, :]), writes=[tm])
        emit("V", lambda e, tm=tm: e.reduce_sum(lnc.a(c0=0, c1=1), tm.a(), axis=AX.X),
             reads=[tm], writes=[lnc])
        emit("V", lambda e: e.tensor_scalar_mul(lnc.a(c0=1, c1=2), lnc.a(c0=0, c1=1), -ILN),
             reads=[lnc], writes=[lnc])
        emit("S", lambda e, tm=tm: e.activation(tms.a(), tm.a(), AF.Identity, bias=lnc.a(c0=1, c1=2)),
             reads=[tm, lnc], writes=[tms])
        emit("V", lambda e, tm=tm: e.scalar_tensor_tensor(
            tm.a(), tms.a(), 1.0, tms.a(), ALU.mult, ALU.mult, accum_out=lnc.a(c0=2, c1=3)),
            reads=[tms], writes=[tm, lnc])
        emit("V", lambda e: e.tensor_scalar(lnc.a(c0=3, c1=4), lnc.a(c0=2, c1=3), ILN, EPS,
                                            ALU.mult, ALU.add), reads=[lnc], writes=[lnc])
        emit("S", lambda e: e.activation(lnc.a(c0=4, c1=5), lnc.a(c0=3, c1=4), AF.Ln),
             reads=[lnc], writes=[lnc])
        emit("S", lambda e: e.activation(lnc.a(c0=5, c1=6), lnc.a(c0=4, c1=5), AF.Exp, scale=-0.5),
             reads=[lnc], writes=[lnc])
        emit("V", lambda e: e.tensor_scalar_mul(tms.a(), tms.a(), lnc.a(c0=5, c1=6)),
             reads=[lnc], writes=[tms])
        for fc in range(8):
            pb = PBS[fc % 2]
            emit("T", lambda e, fc=fc, pb=pb: e.transpose(
                pb[:, 0:128], tms.a(c0=fc * 128, c1=(fc + 1) * 128), k.c_id32[:, :]),
                reads=[tms, k.c_id32], writes=[pb])
            if fc % 2 == 0:
                emit("V", lambda e, fc=fc, pb=pb, tc=tc: e.tensor_copy(
                    k.xf[fc][:, 1 + tc * 128: 1 + (tc + 1) * 128], pb[:, 0:128]),
                    reads=[pb], writes=[k.xf[fc]])
            else:
                emit("S", lambda e, fc=fc, pb=pb, tc=tc: e.copy(
                    k.xf[fc][:, 1 + tc * 128: 1 + (tc + 1) * 128], pb[:, 0:128]),
                    reads=[pb], writes=[k.xf[fc]])


def _retnet(k, b):
    P, emit, pool, PBS = k.P, k.P.emit, k.pool, k.PBS
    _tap(k, "xf_pre", k.xf[0][:, 0:513], [k.xf[0]])
    if "wo" not in RET_SUB:
        for fc in range(8):
            emit("V", lambda e, fc=fc: e.memset(k.acc[fc][:, :], 0.0), writes=[k.acc[fc]])
    if "carve" not in RET_SUB:
        return
    pool.reset()
    rtab = [[pool.take(128, TT, BF16) for _ in range(4)] for _ in range(2)]
    r_q = pool.take(128, TT, BF16)
    r_k = pool.take(128, TT, BF16)
    r_rot = pool.take(128, TT, BF16)
    r_t1 = pool.take(128, TT, BF16)
    r_vag = [pool.take(128, 8 * 257 + 2, BF16) for _ in range(4)]
    r_sct = [pool.take(128, TT, BF16) for _ in range(4)]
    r_ktT = pool.take(128, TT, BF16)
    r_hat = [pool.take(128, 256, BF16) for _ in range(2)]
    r_haf = [pool.take(128, TT, BF16) for _ in range(16)]
    r_sil = pool.take(128, TT, BF16)
    r_sq = pool.take(128, 512, BF16)
    wvv = [pool.take(128, 4096, BF16)]

    for h in range(H_RET if "stmem" in RET_SUB else 0):
        emit("V", lambda e, h=h: e.memset(k.stf[h][:, 0:257], 0.0), writes=[k.stf[h]])

    for tt in range(NTT if "tt" in RET_SUB else 0):
        cr = tt % 2
        if "cast" in RET_SUB:
            _do_cast(k, b, tt, cr)
        cc = k.cast[cr]

        for vc in range(4 if 'vproj' in RET_SUB else 0):
            wv = wvv[0]
            emit("DQ", lambda e, vc=vc, wv=wv: e.dma_start(
                wv.ap3(0, [[512, 8], [1, 512]]),
                bass.AP(k.WB["wv"], vc * 512, [[VDIM, 128], [128 * VDIM, 8], [1, 512]])),
                reads=[k.WBtok["wv"]], writes=[wv])
            for jt in range(4):
                pv = PBS[4 + (jt % 2)]
                for fi in range(8):
                    emit("T", lambda e, fi=fi, jt=jt, wv=wv, pv=pv: e.matmul(
                        pv[:, 0:512], cc[fi][:, 1 + jt * 128: 1 + (jt + 1) * 128],
                        wv.a(c0=fi * 512, c1=(fi + 1) * 512),
                        start=(fi == 0), stop=(fi == 7)),
                        reads=[cc[fi], wv], writes=[pv])
                emit("V", lambda e, jt=jt, vc=vc, pv=pv: e.tensor_copy(
                    r_vag[jt].ap3(257 * 2 * vc, [[257, 2], [1, 256]]),
                    bass.AP(pv, 0, [[512, 128], [256, 2], [1, 256]])),
                    reads=[pv], writes=[r_vag[jt]])
        for jt in range(4 if 'vproj' in RET_SUB else 0):
            emit("V", lambda e, jt=jt: e.memset(
                r_vag[jt].ap3(256, [[257, 8], [1, 1]]), 1.0),
                reads=[r_vag[jt]], writes=[r_vag[jt]])

        for h in range(H_RET if RET_SUB & {'tables', 'qkproj'} else 0):
            ts = rtab[h % 2]
            for i in range(4 if 'tables' in RET_SUB else 0):
                emit("DQ", lambda e, i=i, h=h, tt=tt: e.dma_start(
                    ts[i].a(), k.t_tab[i][h * 128:(h + 1) * 128, tt * TT:(tt + 1) * TT]),
                    writes=[ts[i]])
            if 'qkproj' not in RET_SUB:
                continue
            wq = k.wbuf[0 + (h % 2)]
            wk = k.wbuf[2 + (h % 2)]
            emit("DQ", lambda e, h=h, wq=wq: e.dma_start(
                wq[:, 0:1024].rearrange("p (c n) -> p c n", c=8),
                bass.AP(k.WB["wq"], h * 128, [[D, 128], [128 * D, 8], [1, 128]])),
                reads=[k.WBtok["wq"]], writes=[wq])
            emit("DQ", lambda e, h=h, wk=wk: e.dma_start(
                wk[:, 0:1024].rearrange("p (c n) -> p c n", c=8),
                bass.AP(k.WB["wk"], h * 128, [[D, 128], [128 * D, 8], [1, 128]])),
                reads=[k.WBtok["wk"]], writes=[wk])
            pq, pk, pro = PBS[0], PBS[1], PBS[2]
            for fi in range(8):
                emit("T", lambda e, fi=fi, wq=wq, pq=pq: e.matmul(
                    pq[:, 0:512], wq[:, fi * 128:(fi + 1) * 128], cc[fi][:, 1:1 + TT],
                    start=(fi == 0), stop=(fi == 7)),
                    reads=[wq, cc[fi]], writes=[pq])
            for fi in range(8):
                emit("T", lambda e, fi=fi, wk=wk, pk=pk: e.matmul(
                    pk[:, 0:512], wk[:, fi * 128:(fi + 1) * 128], cc[fi][:, 1:1 + TT],
                    start=(fi == 0), stop=(fi == 7)),
                    reads=[wk, cc[fi]], writes=[pk])

            def rotary(psrc, tcs, tsn, dst):
                emit("V", lambda e: e.tensor_copy(r_rot[:, 0:TT], psrc[:, 0:512]),
                     reads=[psrc], writes=[r_rot])
                emit("T", lambda e: e.matmul(pro[:, 0:512], k.c_pret[:, :], r_rot[:, 0:TT],
                                             start=True, stop=True),
                     reads=[k.c_pret, r_rot], writes=[pro])
                emit("V", lambda e: e.scalar_tensor_tensor(
                    r_t1[:, 0:TT], psrc[:, 0:512], 1.0, tcs.a(), ALU.mult, ALU.mult),
                    reads=[psrc, tcs], writes=[r_t1])
                emit("V", lambda e: e.scalar_tensor_tensor(
                    r_rot[:, 0:TT], pro[:, 0:512], 1.0, tsn.a(), ALU.mult, ALU.mult),
                    reads=[pro, tsn], writes=[r_rot])
                emit("V", lambda e: e.tensor_add(dst[:, 0:TT], r_rot[:, 0:TT], r_t1[:, 0:TT]),
                     reads=[r_t1, r_rot], writes=[dst])

            if 'rot' not in RET_SUB:
                continue
            rotary(pq, ts[0], ts[1], r_q)
            rotary(pk, ts[2], ts[3], r_k)
            if tt == 0 and h == 0 and b == 0:
                _tap(k, "cast0", cc[0][:, 0:TT + 1], [cc[0]])
                _tap(k, "rq", r_q.a(), [r_q])
                _tap(k, "rk", r_k.a(), [r_k])

            for jc in range(4 if 'scores' in RET_SUB else 0):
                psc = PBS[2 + (jc % 2)]
                emit("T", lambda e, jc=jc, psc=psc: e.matmul(
                    psc[:, jc * 128:512], r_k[:, jc * 128:(jc + 1) * 128],
                    r_q[:, jc * 128:TT], start=True, stop=True),
                    reads=[r_k, r_q], writes=[psc])
                emit("V", lambda e, jc=jc, psc=psc: e.tensor_mul(
                    r_sct[jc][:, jc * 128:(jc + 1) * 128],
                    psc[:, jc * 128:(jc + 1) * 128], k.c_mup[:, :]),
                    reads=[psc, k.c_mup], writes=[r_sct[jc]])
                if jc < 3:
                    emit("S", lambda e, jc=jc, psc=psc: e.copy(
                        r_sct[jc][:, (jc + 1) * 128:512], psc[:, (jc + 1) * 128:512]),
                        reads=[psc], writes=[r_sct[jc]])

            if tt == 0 and h == 0 and b == 0:
                _tap(k, "sct0", r_sct[0].a(), [r_sct[0]])
                _tap(k, "sct1", r_sct[1].a(), [r_sct[1]])
                _tap(k, "vag0", r_vag[0].a(), [r_vag[0]])
            pkt = PBS[2]
            for jc in range(4 if 'ktt' in RET_SUB else 0):
                emit("T", lambda e, jc=jc, pkt=pkt: e.matmul(
                    pkt[:, jc * 128:(jc + 1) * 128], r_k[:, jc * 128:(jc + 1) * 128],
                    k.c_idbf[:, :], start=True, stop=True),
                    reads=[r_k, k.c_idbf], writes=[pkt])
            if 'ktt' in RET_SUB:
                emit("V", lambda e, pkt=pkt: e.tensor_copy(r_ktT[:, 0:512], pkt[:, 0:512]),
                     reads=[pkt], writes=[r_ktT])

            if 'value' not in RET_SUB:
                continue
            emit("V", lambda e, h=h: e.tensor_copy(k.sthi[:, 0:257], k.stf[h][:, 0:257]),
                 reads=[k.stf[h]], writes=[k.sthi])
            emit("V", lambda e, h=h: e.tensor_sub(k.stlo[:, 0:257], k.stf[h][:, 0:257],
                                                  k.sthi[:, 0:257]),
                 reads=[k.stf[h], k.sthi], writes=[k.stlo])

            for qc in range(4):
                po = PBS[6 + (qc % 2)]
                for jc in range(qc + 1):
                    emit("T", lambda e, jc=jc, qc=qc, po=po, h=h: e.matmul(
                        po[:, 0:257], r_sct[jc][:, qc * 128:(qc + 1) * 128],
                        r_vag[jc].a(c0=257 * h, c1=257 * h + 257),
                        start=(jc == 0), stop=False),
                        reads=[r_sct[jc], r_vag[jc]], writes=[po])
                emit("T", lambda e, qc=qc, po=po: e.matmul(
                    po[:, 0:257], r_q[:, qc * 128:(qc + 1) * 128], k.sthi[:, 0:257],
                    start=False, stop=False),
                    reads=[r_q, k.sthi], writes=[po])
                emit("T", lambda e, qc=qc, po=po: e.matmul(
                    po[:, 0:257], r_q[:, qc * 128:(qc + 1) * 128], k.stlo[:, 0:257],
                    start=False, stop=True),
                    reads=[r_q, k.stlo], writes=[po])
                et = k.eps_t
                emit("V", lambda e, po=po: e.tensor_scalar_mul(
                    et[:, 10:11], po[:, 256:257], -1.0),
                    reads=[po], writes=[et])
                emit("V", lambda e, po=po: e.scalar_tensor_tensor(
                    et[:, 0:1], et[:, 10:11], 1.0, po[:, 256:257],
                    ALU.max, ALU.max),
                    reads=[po, et], writes=[et])
                emit("V", lambda e: e.reciprocal(et[:, 1:2], et[:, 0:1]),
                     reads=[et], writes=[et])
                emit("S", lambda e, po=po: e.activation(
                    r_sq[:, 0:256], po[:, 0:256], AF.Square, accum_out=et[:, 2:3]),
                    reads=[po], writes=[r_sq, et])
                emit("V", lambda e: e.tensor_scalar(
                    et[:, 3:4], et[:, 2:3], 1.0 / 256.0, None, ALU.mult),
                    reads=[et], writes=[et])
                emit("V", lambda e: e.tensor_mul(et[:, 4:5], et[:, 3:4], et[:, 1:2]),
                     reads=[et], writes=[et])
                emit("V", lambda e: e.tensor_mul(et[:, 5:6], et[:, 4:5], et[:, 1:2]),
                     reads=[et], writes=[et])
                emit("V", lambda e: e.tensor_scalar(
                    et[:, 6:7], et[:, 5:6], 1.0, EPS, ALU.mult, ALU.add),
                    reads=[et], writes=[et])
                emit("S", lambda e: e.activation(et[:, 7:8], et[:, 6:7], AF.Ln),
                     reads=[et], writes=[et])
                emit("S", lambda e: e.activation(et[:, 8:9], et[:, 7:8], AF.Exp, scale=-0.5),
                     reads=[et], writes=[et])
                emit("V", lambda e: e.tensor_mul(et[:, 9:10], et[:, 8:9], et[:, 1:2]),
                     reads=[et], writes=[et])
                hb = r_hat[qc % 2]
                emit("V", lambda e, po=po, hb=hb: e.tensor_scalar(
                    hb.a(), po[:, 0:256], et[:, 9:10], None, ALU.mult),
                    reads=[po, et], writes=[hb])
                for blk in range(2):
                    pt = PBS[4 + blk]
                    emit("T", lambda e, blk=blk, qc=qc, hb=hb, pt=pt: e.matmul(
                        pt[:, qc * 128:(qc + 1) * 128],
                        hb.a(c0=blk * 128, c1=(blk + 1) * 128), k.c_idbf[:, :],
                        start=True, stop=True),
                        reads=[hb, k.c_idbf], writes=[pt])
            for blk in range(2):
                pt = PBS[4 + blk]
                emit("V", lambda e, blk=blk, pt=pt, h=h: e.tensor_copy(
                    r_haf[2 * h + blk].a(), pt[:, 0:512]),
                    reads=[pt], writes=[r_haf[2 * h + blk]])
            if tt == 0 and h == 0 and b == 0:
                _tap(k, "haf0", r_haf[0].a(), [r_haf[0]])
                _tap(k, "ktT", r_ktT.a(), [r_ktT])

            pkv = PBS[5]
            for jc in range(4 if 'state' in RET_SUB else 0):
                emit("T", lambda e, jc=jc, h=h, pkv=pkv: e.matmul(
                    pkv[:, 0:257], r_ktT[:, jc * 128:(jc + 1) * 128],
                    r_vag[jc].a(c0=257 * h, c1=257 * h + 257),
                    start=(jc == 0), stop=(jc == 3)),
                    reads=[r_ktT, r_vag[jc]], writes=[pkv])
            if 'state' in RET_SUB:
                emit("V", lambda e, h=h, pkv=pkv: e.tensor_add(
                    k.stf[h][:, 0:257], k.stf[h][:, 0:257], pkv[:, 0:257]),
                    reads=[pkv, k.stf[h]], writes=[k.stf[h]])
            if tt == 0 and h == 0 and b == 0:
                _tap(k, "st0", k.stf[0][:, 0:257], [k.stf[0]])

        for vd in range(16 if 'silu' in RET_SUB else 0):
            wgv = k.wbuf[vd % 2]
            emit("DQ", lambda e, vd=vd, wgv=wgv: e.dma_start(
                wgv[:, 0:1024].rearrange("p (c n) -> p c n", c=8),
                bass.AP(k.WB["wg"], vd * 128, [[VDIM, 128], [128 * VDIM, 8], [1, 128]])),
                reads=[k.WBtok["wg"]], writes=[wgv])
            pg = PBS[0 + (vd % 2)]
            for fi in range(8):
                emit("T", lambda e, fi=fi, wgv=wgv, pg=pg: e.matmul(
                    pg[:, 0:512], wgv[:, fi * 128:(fi + 1) * 128], cc[fi][:, 1:1 + TT],
                    start=(fi == 0), stop=(fi == 7)),
                    reads=[wgv, cc[fi]], writes=[pg])
            emit("S", lambda e, pg=pg: e.activation(r_sil[:, 0:TT], pg[:, 0:512], AF.Silu),
                 reads=[pg], writes=[r_sil])
            emit("V", lambda e, vd=vd: e.tensor_mul(
                r_haf[vd].a(), r_haf[vd].a(), r_sil[:, 0:TT]),
                reads=[r_sil, r_haf[vd]], writes=[r_haf[vd]])


        for fc in range(8 if 'wo' in RET_SUB else 0):
            wov = k.wbig[fc % 2]
            emit("DQ", lambda e, fc=fc, wov=wov: e.dma_start(
                wov[:, 0:2048].rearrange("p (c n) -> p c n", c=16),
                bass.AP(k.WB["wo"], fc * 128, [[D, 128], [128 * D, 16], [1, 128]])),
                reads=[k.WBtok["wo"]], writes=[wov])
            pw = PBS[2 + (fc % 2)]
            for vd in range(16):
                emit("T", lambda e, vd=vd, wov=wov, pw=pw: e.matmul(
                    pw[:, 0:512], wov[:, vd * 128:(vd + 1) * 128], r_haf[vd].a(),
                    start=(vd == 0), stop=(vd == 15)),
                    reads=[wov, r_haf[vd]], writes=[pw])
            emit("V", lambda e, fc=fc, tt=tt, pw=pw: e.tensor_copy(
                k.acc[fc][:, tt * TT:(tt + 1) * TT], pw[:, 0:512]),
                reads=[pw], writes=[k.acc[fc]])
        if tt == 0 and b == 0:
            _tap(k, "acc0", k.acc[0][:, 0:TT], [k.acc[0]])
            _tap(k, "sil15", r_sil.a(), [r_sil])


    _tap(k, "xf_post", k.xf[0][:, 0:513], [k.xf[0]])
    _tap(k, "stf7", k.stf[7][:, 0:257], [k.stf[7]])


def _ffn_layer(k, b, l):
    P, emit, pool, PBS = k.P, k.P.emit, k.pool, k.PBS
    pool.reset()
    f_qr = [pool.take(128, TT, BF16) for _ in range(4)]
    f_kr = [pool.take(128, 2 * TT, BF16) for _ in range(4)]
    f_vag = [pool.take(128, 8 * 65 + 2, BF16) for _ in range(8)]
    f_pt = [pool.take(128, TT, BF16) for _ in range(2)]
    f_o = [pool.take(128, TT, BF16) for _ in range(4)]
    f_sig = pool.take(16, TT, F32)
    f_row = pool.take(1, TT, F32)
    f_rot = pool.take(128, TT, BF16)
    f_t1 = pool.take(128, TT, BF16)
    f_sq = pool.take(128, TT, BF16)
    f_bc = pool.take(64, TT, BF16)
    f_ot = pool.take(64, TT, BF16)
    wvv = pool.take(128, 4096, BF16)
    tabt = pool.take(128, TT, BF16)
    ffold = [pool.take(128, TT, BF16) for _ in range(4)]

    for tt in range(NTT):
        cr = tt % 2
        if tt == 0:
            _do_cast(k, b, 0, 0)
        cc = k.cast[cr]

        for i in range(4 if 'fold' in FFN_SUB else 0):
            emit("DQ", lambda e, i=i, tt=tt: e.dma_start(
                tabt.a(), k.f_tab[i][:, tt * TT:(tt + 1) * TT]),
                writes=[tabt])
            emit("V", lambda e, i=i: e.tensor_scalar_mul(
                ffold[i].a(), tabt.a(), k.c_qs[:, l * 4 + i: l * 4 + i + 1]),
                reads=[tabt, k.c_qs], writes=[ffold[i]])

        if 'vproj' not in FFN_SUB:
            continue
        emit("DQ", lambda e: e.dma_start(
            wvv.ap3(0, [[512, 8], [1, 512]]),
            bass.AP(k.WB["wqkv"], l * D * 3 * INNER + 2 * INNER,
                    [[3 * INNER, 128], [128 * 3 * INNER, 8], [1, 512]])),
            reads=[k.WBtok["wqkv"]], writes=[wvv])
        for jt in range(4):
            slot = (tt * 4 + jt) % 8
            pv = PBS[4 + (jt % 2)]
            for fi in range(8):
                sh = 0 if fi < 4 else -1
                emit("T", lambda e, fi=fi, jt=jt, sh=sh, pv=pv: e.matmul(
                    pv[:, 0:512], cc[fi][:, 1 + sh + jt * 128: 1 + sh + (jt + 1) * 128],
                    wvv.a(c0=fi * 512, c1=(fi + 1) * 512),
                    start=(fi == 0), stop=(fi == 7)),
                    reads=[cc[fi], wvv], writes=[pv])
            emit("V", lambda e, slot=slot, pv=pv: e.tensor_copy(
                f_vag[slot].ap3(0, [[65, 8], [1, 64]]),
                bass.AP(pv, 0, [[512, 128], [64, 8], [1, 64]])),
                reads=[pv], writes=[f_vag[slot]])
            emit("V", lambda e, slot=slot: e.memset(
                f_vag[slot].ap3(64, [[65, 8], [1, 1]]), 1.0),
                reads=[f_vag[slot]], writes=[f_vag[slot]])

        pn = PBS[3]
        for fo in range(8 if 'qk' in FFN_SUB else 0):
            wql = k.wbuf[fo % 2]
            emit("DQ", lambda e, fo=fo, wql=wql: e.dma_start(
                wql[:, 0:1024].rearrange("p (c n) -> p c n", c=8),
                bass.AP(k.WB["wqkv"], l * D * 3 * INNER + fo * 128,
                        [[3 * INNER, 128], [128 * 3 * INNER, 8], [1, 128]])),
                reads=[k.WBtok["wqkv"]], writes=[wql])
            pq = PBS[0 + (fo % 2)]
            for fi in range(8):
                sh = 0 if fi < 4 else -1
                emit("T", lambda e, fi=fi, sh=sh, wql=wql, pq=pq: e.matmul(
                    pq[:, 0:512], wql[:, fi * 128:(fi + 1) * 128],
                    cc[fi][:, 1 + sh: 1 + sh + TT],
                    start=(fi == 0), stop=(fi == 7)),
                    reads=[wql, cc[fi]], writes=[pq])
            if 'qknorm' in FFN_SUB:
                emit("S", lambda e, pq=pq: e.activation(f_sq.a(), pq[:, 0:512], AF.Square),
                     reads=[pq], writes=[f_sq])
                emit("T", lambda e, fo=fo, pn=pn: e.matmul(
                    pn[0:16, 0:512], k.c_ind[fo][:, :], f_sq.a(),
                    start=(fo == 0), stop=(fo == 7)),
                    reads=[k.c_ind[fo], f_sq], writes=[pn])
            if 'qkrot' not in FFN_SUB:
                continue
            tcs = ffold[0] if fo < 4 else ffold[2]
            tsn = ffold[1] if fo < 4 else ffold[3]
            pro = PBS[2]
            emit("V", lambda e, pq=pq: e.tensor_copy(f_rot.a(), pq[:, 0:512]),
                 reads=[pq], writes=[f_rot])
            emit("T", lambda e, pro=pro: e.matmul(pro[:, 0:512], k.c_pffn[:, :], f_rot.a(),
                                                  start=True, stop=True),
                 reads=[k.c_pffn, f_rot], writes=[pro])
            emit("V", lambda e, pq=pq, tcs=tcs: e.scalar_tensor_tensor(
                f_t1.a(), pq[:, 0:512], 1.0, tcs.a(), ALU.mult, ALU.mult),
                reads=[pq, tcs], writes=[f_t1])
            emit("V", lambda e, pro=pro, tsn=tsn: e.scalar_tensor_tensor(
                f_rot.a(), pro[:, 0:512], 1.0, tsn.a(), ALU.mult, ALU.mult),
                reads=[pro, tsn], writes=[f_rot])
            dst = f_qr[fo] if fo < 4 else f_kr[fo - 4]
            co = 0 if fo < 4 else (tt % 2) * TT
            emit("V", lambda e, dst=dst, co=co: e.tensor_add(
                dst.a(c0=co, c1=co + TT), f_rot.a(), f_t1.a()),
                reads=[f_rot, f_t1], writes=[dst])
        if 'qknorm' not in FFN_SUB:
            continue
        emit("V", lambda e, pn=pn: e.tensor_scalar_max(f_sig.a(0, 16), pn[0:16, 0:512], 1e-24),
             reads=[pn], writes=[f_sig])
        emit("S", lambda e: e.activation(f_sig.a(0, 16), f_sig.a(0, 16), AF.Ln),
             reads=[f_sig], writes=[f_sig])
        emit("S", lambda e: e.activation(f_sig.a(0, 16), f_sig.a(0, 16), AF.Exp, scale=-0.5),
             reads=[f_sig], writes=[f_sig])
        for fo in range(8 if 'sigb' in FFN_SUB else 0):
            pb2 = PBS[2]
            emit("T", lambda e, fo=fo, pb2=pb2: e.matmul(
                pb2[:, 0:512], k.c_indT[fo][:, :], f_sig.a(0, 16), start=True, stop=True),
                reads=[k.c_indT[fo], f_sig], writes=[pb2])
            dst = f_qr[fo] if fo < 4 else f_kr[fo - 4]
            co = 0 if fo < 4 else (tt % 2) * TT
            emit("V", lambda e, dst=dst, co=co, pb2=pb2: e.scalar_tensor_tensor(
                dst.a(c0=co, c1=co + TT), pb2[:, 0:512], 1.0, dst.a(c0=co, c1=co + TT),
                ALU.mult, ALU.mult),
                reads=[pb2, dst], writes=[dst])

        if b == 0 and l == 0 and tt == 0:
            _tap(k, "fqr0", f_qr[0].a(), [f_qr[0]])
            _tap(k, "fkr0", f_kr[0].a(c0=0, c1=TT), [f_kr[0]])
            _tap(k, "fvag0", f_vag[0].a(), [f_vag[0]])
            _tap(k, "fsig0", f_sig.a(0, 16), [f_sig])
        if tt + 1 < NTT:
            _do_cast(k, b, tt + 1, (tt + 1) % 2)

        for h in range(H_RET if 'attn' in FFN_SUB else 0):
            fo2 = h // 2
            pb_ = 64 * (h % 2)
            po = PBS[6 + (h % 2)]
            first = True
            stages = []
            for i in range(8):
                if tt == 0 and i < 4:
                    continue
                tile = tt - 1 + (i // 4)
                kco = (tile % 2) * TT + (i % 4) * 128
                if i < 4:
                    q0, q1 = 0, 128 * (i + 1)
                else:
                    q0, q1 = 128 * (i - 4), 512
                stages.append((i, tile, kco, q0, q1))

            def do_sim(st):
                i, tile, kco, q0, q1 = st
                psim = PBS[2 + (i % 2)]
                emit("T", lambda e, fo2=fo2, pb_=pb_, kco=kco, q0=q0, q1=q1, psim=psim: e.matmul(
                    psim[:, q0:q1],
                    f_kr[fo2].a(p0=pb_, p1=pb_ + 64, c0=kco, c1=kco + 128),
                    f_qr[fo2].a(p0=pb_, p1=pb_ + 64, c0=q0, c1=q1),
                    start=True, stop=True),
                    reads=[f_kr[fo2], f_qr[fo2]], writes=[psim])

            def do_rest(st, first):
                i, tile, kco, q0, q1 = st
                psim = PBS[2 + (i % 2)]
                pt = f_pt[i % 2]
                emit("S", lambda e, q0=q0, q1=q1, psim=psim, pt=pt: e.activation(
                    pt.a(c0=q0, c1=q1), psim[:, q0:q1], AF.Exp),
                    reads=[psim], writes=[pt])
                if i < 4:
                    mb = 128 * i
                    emit("V", lambda e, mb=mb, pt=pt: e.tensor_mul(
                        pt.a(c0=mb, c1=mb + 128), pt.a(c0=mb, c1=mb + 128), k.c_mlo[:, :]),
                        reads=[k.c_mlo, pt], writes=[pt])
                else:
                    mb = 128 * (i - 4)
                    emit("V", lambda e, mb=mb, pt=pt: e.tensor_mul(
                        pt.a(c0=mb, c1=mb + 128), pt.a(c0=mb, c1=mb + 128), k.c_mup[:, :]),
                        reads=[k.c_mup, pt], writes=[pt])
                slot = (tile * 4 + (i % 4)) % 8
                emit("T", lambda e, slot=slot, h=h, q0=q0, q1=q1, po=po, pt=pt, first=first:
                     e.matmul(po[0:65, q0:q1], f_vag[slot].a(c0=65 * h, c1=65 * h + 65),
                              pt.a(c0=q0, c1=q1), start=first, stop=False),
                     reads=[f_vag[slot], pt], writes=[po])

            prev = None
            for st in stages:
                do_sim(st)
                if prev is not None:
                    do_rest(prev, first)
                    first = False
                prev = st
            do_rest(prev, first)
            first = False
            emit("V", lambda e, po=po: e.reciprocal(f_row.a(0, 1), po[64:65, 0:512]),
                 reads=[po], writes=[f_row])
            pb2 = PBS[5]
            emit("T", lambda e, pb2=pb2: e.matmul(
                pb2[0:64, 0:512], k.c_o64f[:, :], f_row.a(0, 1), start=True, stop=True),
                reads=[k.c_o64f, f_row], writes=[pb2])
            emit("V", lambda e, pb2=pb2: e.tensor_copy(f_bc.a(0, 64), pb2[0:64, 0:512]),
                 reads=[pb2], writes=[f_bc])
            if h % 2 == 0:
                emit("V", lambda e, h=h, po=po: e.tensor_mul(
                    f_o[h // 2].a(p0=0, p1=64), po[0:64, 0:512], f_bc.a(0, 64)),
                    reads=[po, f_bc], writes=[f_o[h // 2]])
            else:
                emit("V", lambda e, po=po: e.tensor_mul(
                    f_ot.a(0, 64), po[0:64, 0:512], f_bc.a(0, 64)),
                    reads=[po, f_bc], writes=[f_ot])
                psh = PBS[4]
                emit("T", lambda e, psh=psh: e.matmul(
                    psh[:, 0:512], k.c_sh64[:, :], f_ot.a(0, 64), start=True, stop=True),
                    reads=[k.c_sh64, f_ot], writes=[psh])
                emit("V", lambda e, h=h, psh=psh: e.tensor_copy(
                    f_o[h // 2].a(p0=64, p1=128), psh[64:128, 0:512]),
                    reads=[psh], writes=[f_o[h // 2]])

        if b == 0 and l == 0 and tt == 0:
            _tap(k, "fo0", f_o[0].a(), [f_o[0]])
        for fc in range(8 if 'wo' in FFN_SUB else 0):
            wofv = k.wbuf[2 + (fc % 2)]
            emit("DQ", lambda e, fc=fc, wofv=wofv: e.dma_start(
                wofv[:, 0:512].rearrange("p (c n) -> p c n", c=4),
                bass.AP(k.WB["wof"], l * INNER * D + fc * 128,
                        [[D, 128], [128 * D, 4], [1, 128]])),
                reads=[k.WBtok["wof"]], writes=[wofv])
            pw = PBS[0 + (fc % 2)]
            for ic in range(4):
                emit("T", lambda e, ic=ic, wofv=wofv, pw=pw: e.matmul(
                    pw[:, 0:512], wofv[:, ic * 128:(ic + 1) * 128], f_o[ic].a(),
                    start=(ic == 0), stop=(ic == 3)),
                    reads=[wofv, f_o[ic]], writes=[pw])
            emit("V", lambda e, fc=fc, tt=tt, pw=pw: e.scalar_tensor_tensor(
                k.xf[fc][:, 1 + tt * TT: 1 + (tt + 1) * TT], pw[:, 0:512], 1.0,
                k.xf[fc][:, 1 + tt * TT: 1 + (tt + 1) * TT], ALU.mult, ALU.add),
                reads=[pw, k.xf[fc]], writes=[k.xf[fc]])

    if b == 0 and l == 0:
        for fi in range(8):
            _tap(k, f"xf_mid{fi}", k.xf[fi][:, 0:2052], [k.xf[fi]])
    pool.reset()
    f_hln = [pool.take(128, TT, BF16) for _ in range(8)]
    f_ag = [pool.take(128, TT, BF16) for _ in range(NHC)]
    f_gel = pool.take(128, TT, BF16)
    f_sq2 = pool.take(128, TT, BF16)
    f_tmp = pool.take(128, TT, BF16)
    rowA = pool.take(1, TT, F32)
    rowC = pool.take(1, TT, F32)
    rowV = pool.take(1, TT, F32)
    rowM = pool.take(1, TT, F32)
    w2v = [pool.take(128, 1024, BF16) for _ in range(2)]

    for tq in (reversed(range(NTT)) if FFN_SUB & {'ln', 'w1', 'w2'} else []):
        cr = tq % 2
        _do_cast(k, b, tq, cr)
        cc = k.cast[cr]
        pmu, pms = PBS[6], PBS[7]
        for fi in range(8):
            sh = 0 if fi < 4 else -1
            emit("T", lambda e, fi=fi, sh=sh, pmu=pmu: e.matmul(
                pmu[0:1, 0:512], k.c_o128b[:, :], cc[fi][:, 1 + sh: 1 + sh + TT],
                start=(fi == 0), stop=(fi == 7)),
                reads=[k.c_o128b, cc[fi]], writes=[pmu])
        for fi in range(8):
            sh = 0 if fi < 4 else -1
            emit("S", lambda e, fi=fi, sh=sh: e.activation(
                f_sq2.a(), cc[fi][:, 1 + sh: 1 + sh + TT], AF.Square),
                reads=[cc[fi]], writes=[f_sq2])
            emit("T", lambda e, fi=fi, pms=pms: e.matmul(
                pms[0:1, 0:512], k.c_o128b[:, :], f_sq2.a(),
                start=(fi == 0), stop=(fi == 7)),
                reads=[k.c_o128b, f_sq2], writes=[pms])
        emit("V", lambda e, pmu=pmu: e.tensor_scalar_mul(rowA.a(0, 1), pmu[0:1, 0:512], 1.0 / D),
             reads=[pmu], writes=[rowA])
        emit("V", lambda e: e.tensor_mul(rowC.a(0, 1), rowA.a(0, 1), rowA.a(0, 1)),
             reads=[rowA], writes=[rowC])
        emit("V", lambda e, pms=pms: e.scalar_tensor_tensor(
            rowV.a(0, 1), pms[0:1, 0:512], 1.0 / D, rowC.a(0, 1), ALU.mult, ALU.subtract),
            reads=[pms, rowC], writes=[rowV])
        emit("V", lambda e: e.tensor_scalar_add(rowV.a(0, 1), rowV.a(0, 1), EPS),
             reads=[rowV], writes=[rowV])
        emit("S", lambda e: e.activation(rowV.a(0, 1), rowV.a(0, 1), AF.Ln),
             reads=[rowV], writes=[rowV])
        emit("S", lambda e: e.activation(rowV.a(0, 1), rowV.a(0, 1), AF.Exp, scale=-0.5),
             reads=[rowV], writes=[rowV])
        emit("V", lambda e: e.tensor_mul(rowM.a(0, 1), rowA.a(0, 1), rowV.a(0, 1)),
             reads=[rowA, rowV], writes=[rowM])
        pbR, pbM = PBS[4], PBS[5]
        emit("T", lambda e, pbR=pbR: e.matmul(
            pbR[:, 0:512], k.c_o128f[:, :], rowV.a(0, 1), start=True, stop=True),
            reads=[k.c_o128f, rowV], writes=[pbR])
        emit("T", lambda e, pbM=pbM: e.matmul(
            pbM[:, 0:512], k.c_o128f[:, :], rowM.a(0, 1), start=True, stop=True),
            reads=[k.c_o128f, rowM], writes=[pbM])
        for fi in range(8):
            sh = 0 if fi < 4 else -1
            emit("V", lambda e, fi=fi, sh=sh, pbR=pbR: e.scalar_tensor_tensor(
                f_tmp.a(), cc[fi][:, 1 + sh: 1 + sh + TT], 1.0, pbR[:, 0:512],
                ALU.mult, ALU.mult),
                reads=[cc[fi], pbR], writes=[f_tmp])
            emit("V", lambda e, pbM=pbM: e.tensor_sub(f_tmp.a(), f_tmp.a(), pbM[:, 0:512]),
                 reads=[pbM, f_tmp], writes=[f_tmp])
            emit("S", lambda e, fi=fi: e.activation(
                f_hln[fi].a(), f_tmp.a(), AF.Identity,
                scale=k.c_lng[:, l * 8 + fi: l * 8 + fi + 1],
                bias=k.c_lnb[:, l * 8 + fi: l * 8 + fi + 1]),
                reads=[f_tmp, k.c_lng, k.c_lnb], writes=[f_hln[fi]])

        if b == 0 and l == 0 and tq == 3:
            _tap(k, "fhln0", f_hln[0].a(), [f_hln[0]])
        if b == 0 and l == 0 and tq == 0:
            _tap(k, "fhln0_t0", f_hln[0].a(), [f_hln[0]])
            _tap(k, "fhln7_t0", f_hln[7].a(), [f_hln[7]])
        for hp in range(NHC if 'w1' in FFN_SUB else 0):
            hw = 128 if hp < NHC - 1 else HID - 128 * (NHC - 1)
            wav, wgv = k.wbuf[0 + (hp % 2)], k.wbuf[2 + (hp % 2)]
            for wsb, wsrc in ((wav, k.WB["wa"]), (wgv, k.WB["wg1"])):
                emit("DQ", lambda e, hp=hp, hw=hw, wsb=wsb, wsrc=wsrc: e.dma_start(
                    wsb[:, 0:8 * hw].rearrange("p (c n) -> p c n", c=8),
                    bass.AP(wsrc, l * D * HID + hp * 128,
                            [[HID, 128], [128 * HID, 8], [1, hw]])),
                    reads=[k.WBtok["wa" if wsrc is k.WB["wa"] else "wg1"]], writes=[wsb])
            pA, pG = PBS[0 + (hp % 2)], PBS[2 + (hp % 2)]
            for fi in range(8):
                emit("T", lambda e, fi=fi, hw=hw, wav=wav, pA=pA: e.matmul(
                    pA[0:hw, 0:512], wav[:, fi * hw:(fi + 1) * hw], f_hln[fi].a(),
                    start=(fi == 0), stop=(fi == 7)),
                    reads=[wav, f_hln[fi]], writes=[pA])
            for fi in range(8):
                emit("T", lambda e, fi=fi, hw=hw, wgv=wgv, pG=pG: e.matmul(
                    pG[0:hw, 0:512], wgv[:, fi * hw:(fi + 1) * hw], f_hln[fi].a(),
                    start=(fi == 0), stop=(fi == 7)),
                    reads=[wgv, f_hln[fi]], writes=[pG])
            emit("S", lambda e, hp=hp, hw=hw, pG=pG: e.activation(
                f_gel.a(p1=hw), pG[0:hw, 0:512], AF.Gelu,
                bias=k.c_b1g[0:hw, l * NHC + hp: l * NHC + hp + 1]),
                reads=[pG, k.c_b1g], writes=[f_gel])
            emit("V", lambda e, hp=hp, hw=hw, pA=pA: e.scalar_tensor_tensor(
                f_ag[hp].a(p1=hw), pA[0:hw, 0:512],
                k.c_b1a[0:hw, l * NHC + hp: l * NHC + hp + 1], f_gel.a(p1=hw),
                ALU.add, ALU.mult),
                reads=[pA, k.c_b1a, f_gel], writes=[f_ag[hp]])

        if 'w2' in FFN_SUB:
            for hp in range(NHC):
                hw = 128 if hp < NHC - 1 else HID - 128 * (NHC - 1)
                wv2 = w2v[hp % 2]
                emit("DQ", lambda e, hp=hp, hw=hw, wv2=wv2: e.dma_start(
                    wv2.a(0, hw, 0, 1024),
                    k.WB["w2"][l][128 * hp:128 * hp + hw, 0:1024]),
                    reads=[k.WBtok["w2"]], writes=[wv2])
                for fc in range(8):
                    pw = PBS[fc]
                    emit("T", lambda e, hp=hp, hw=hw, fc=fc, wv2=wv2, pw=pw: e.matmul(
                        pw[:, 0:512],
                        wv2.a(0, hw, fc * 128, fc * 128 + 128),
                        f_ag[hp].a(p1=hw),
                        start=(hp == 0), stop=(hp == NHC - 1)),
                        reads=[wv2, f_ag[hp]], writes=[pw])
            for fc in range(8):
                pw = PBS[fc]
                emit("V", lambda e, fc=fc, tq=tq, pw=pw: e.scalar_tensor_tensor(
                    k.xf[fc][:, 1 + tq * TT: 1 + (tq + 1) * TT], pw[:, 0:512],
                    k.c_b2[:, l * 8 + fc: l * 8 + fc + 1],
                    k.xf[fc][:, 1 + tq * TT: 1 + (tq + 1) * TT], ALU.add, ALU.add),
                    reads=[pw, k.c_b2, k.xf[fc]], writes=[k.xf[fc]])


def _tap_xf_final(k, b, l):
    if b == 0 and l == 0:
        for fi in range(8):
            _tap(k, f"xf_l0_{fi}", k.xf[fi][:, 0:2052], [k.xf[fi]])


def _assemble(k, b):
    P, emit, pool, PBS = k.P, k.P.emit, k.pool, k.PBS
    pool.reset()
    tmx = [pool.take(128, D, F32) for _ in range(2)]
    outb = [pool.take(128, D, F32) for _ in range(2)]
    for tc in range(16):
        tm = tmx[tc % 2]; ob = outb[tc % 2]
        emit("DQ", lambda e, tm=tm, tc=tc: e.dma_start(
            tm.a(), k.x_in[b, tc * 128:(tc + 1) * 128, :]), writes=[tm])
        base = 4 * (tc % 2)
        pxa, pxb = PBS[base], PBS[base + 1]
        paa, pab = PBS[base + 2], PBS[base + 3]
        for fc in range(8):
            pp = pxa if fc < 4 else pxb
            co = (fc % 4) * 128
            emit("T", lambda e, fc=fc, pp=pp, co=co, tc=tc: e.transpose(
                pp[:, co:co + 128], k.xf[fc][:, 1 + tc * 128: 1 + (tc + 1) * 128],
                k.c_id32[:, :]),
                reads=[k.xf[fc], k.c_id32], writes=[pp])
            if STAGES["attn"]:
                pq = paa if fc < 4 else pab
                emit("T", lambda e, fc=fc, pq=pq, co=co, tc=tc: e.matmul(
                    pq[:, co:co + 128], k.acc[fc][:, tc * 128:(tc + 1) * 128],
                    k.c_idbf[:, :], start=True, stop=True),
                    reads=[k.acc[fc], k.c_idbf], writes=[pq])
        emit("V", lambda e, tm=tm, ob=ob, pxa=pxa: e.tensor_add(
            ob.a(c0=0, c1=512), tm.a(c0=0, c1=512), pxa[:, 0:512]),
            reads=[tm, pxa], writes=[ob])
        emit("V", lambda e, tm=tm, ob=ob, pxb=pxb: e.tensor_add(
            ob.a(c0=512, c1=1024), tm.a(c0=512, c1=1024), pxb[:, 0:512]),
            reads=[tm, pxb], writes=[ob])
        if STAGES["attn"]:
            emit("V", lambda e, ob=ob, paa=paa: e.tensor_add(
                ob.a(c0=0, c1=512), ob.a(c0=0, c1=512), paa[:, 0:512]),
                reads=[paa, ob], writes=[ob])
            emit("V", lambda e, ob=ob, pab=pab: e.tensor_add(
                ob.a(c0=512, c1=1024), ob.a(c0=512, c1=1024), pab[:, 0:512]),
                reads=[pab, ob], writes=[ob])
        emit("GQ", lambda e, ob=ob, tc=tc: e.dma_start(
            k.out_d[b, tc * 128:(tc + 1) * 128, :], ob.a()), reads=[ob],
            dram_store=True)



def _host_prep(inputs):
    tq, tqs, tk, tks = _ret_tables()
    fq = _ffn_tables()
    cn = _consts()
    w1 = np.asarray(inputs["L0_ffn_w1"])
    b1 = np.asarray(inputs["L0_ffn_b1"])
    qs_i = np.asarray(inputs["L0_ffn_q_scale"], dtype=np.float32)
    ks_i = np.asarray(inputs["L0_ffn_k_scale"], dtype=np.float32)

    def rep(v):
        return np.concatenate([v, v], axis=1)

    def sw(v):
        return np.concatenate([v[:, 32:64], v[:, 0:32]], axis=1)

    qs_all = np.stack([rep(qs_i), rep(sw(qs_i)), rep(ks_i), rep(sw(ks_i))],
                      axis=2).astype(np.float32)

    def colpad(v, nch):
        out = np.zeros((2, 128, nch), dtype=np.float32)
        kk = v.shape[1]
        for c in range(nch):
            w = min(128, kk - c * 128)
            out[:, 0:w, c] = v[:, c * 128:c * 128 + w]
        return out

    common = {
        "wq": np.ascontiguousarray(inputs["L0_attn_wq"], dtype=np.float32),
        "wk": np.ascontiguousarray(inputs["L0_attn_wk"], dtype=np.float32),
        "wv": np.ascontiguousarray(inputs["L0_attn_wv"], dtype=np.float32),
        "wg": np.ascontiguousarray(inputs["L0_attn_wg"], dtype=np.float32),
        "wo": np.ascontiguousarray(inputs["L0_attn_wo"], dtype=np.float32),
        "wqkv": np.ascontiguousarray(inputs["L0_ffn_wqkv"], dtype=np.float32),
        "wof": np.ascontiguousarray(inputs["L0_ffn_wo"], dtype=np.float32),
        "wa": np.ascontiguousarray(w1[:, :, 0:HID], dtype=np.float32),
        "wg1": np.ascontiguousarray(w1[:, :, HID:], dtype=np.float32),
        "w2": np.ascontiguousarray(inputs["L0_ffn_w2"], dtype=np.float32),
        "qs_c": qs_all,
        "lng_c": colpad(np.asarray(inputs["L0_ffn_ln_g"], dtype=np.float32), 8),
        "lnb_c": colpad(np.asarray(inputs["L0_ffn_ln_b"], dtype=np.float32), 8),
        "b1a_c": colpad(np.ascontiguousarray(b1[:, 0:HID]).astype(np.float32), NHC),
        "b1g_c": colpad(np.ascontiguousarray(b1[:, HID:]).astype(np.float32), NHC),
        "b2_c": colpad(np.asarray(inputs["L0_ffn_b2"], dtype=np.float32), 8),
        "t_tab0": tq, "t_tab1": tqs, "t_tab2": tk, "t_tab3": tks,
        "f_tab0": fq[0], "f_tab1": fq[1], "f_tab2": fq[2], "f_tab3": fq[3],
    }
    common.update(cn)
    x = np.asarray(inputs["L0_attn_x"], dtype=np.float32)
    return [dict(common, x=np.ascontiguousarray(x[2 * c:2 * c + 2])) for c in range(NCORES)]


def kernel(**inputs):
    maps = _host_prep(inputs)
    k = build_program()
    res = run_bass_kernel_spmd(k.nc, maps, core_ids=list(range(NCORES)))
    return np.concatenate([r["out"] for r in res.results], axis=0).astype(np.float32)


# version 08
# speedup vs baseline: 1.0352x; 1.0352x over previous
import math
import numpy as np
import ml_dtypes
import concourse.bass as bass
import concourse.mybir as mybir
from concourse.bass_utils import run_bass_kernel_spmd

F32 = mybir.dt.float32
BF16 = mybir.dt.bfloat16
U8 = mybir.dt.uint8
AF = mybir.ActivationFunctionType
ALU = mybir.AluOpType
AX = mybir.AxisListType
BF = ml_dtypes.bfloat16

B_TOT, S, D = 16, 2048, 1024
B, NCORES = 2, 8
H_RET, KD, VD_H, VDIM = 8, 128, 256, 2048
DH, INNER, WINDOW, HID, NHC = 64, 512, 512, 2730, 22
TT, NTT = 512, 4
EPS = 1e-5

STAGES = dict(attn=True, ffn=True)
FFN_SUB = {'fold', 'vproj', 'qk', 'qknorm', 'qkrot', 'sigb', 'attn', 'wo', 'ln', 'w1', 'w2'}
RET_SUB = {'carve', 'stmem', 'tt', 'cast', 'vproj', 'tables', 'qkproj', 'rot', 'scores', 'ktt', 'value', 'state', 'silu', 'wo'}
TAP = False


def _tap(k, name, ap, reads):
    if not TAP or name in k.taps:
        return
    shp = [int(s) for s in np.asarray(ap.shape)] if hasattr(ap, "shape") else None
    t = k.nc.dram_tensor("tap_" + name, list(ap.shape), ap.dtype, kind="ExternalOutput")
    k.taps[name] = t
    k.P.emit("DQ", lambda e: e.dma_start(t[:, :] if len(t.shape) == 2 else t[:], ap),
             reads=reads)



def _ret_tables():
    t = np.arange(S, dtype=np.float64)
    angle = np.repeat(1.0 / 10000.0 ** np.linspace(0.0, 1.0, KD // 2), 2)
    cos = np.cos(t[None, :] * angle[:, None])
    sin = np.sin(t[None, :] * angle[:, None])
    qc = np.zeros((H_RET * KD, S)); qs = np.zeros((H_RET * KD, S))
    kc = np.zeros((H_RET * KD, S)); ks = np.zeros((H_RET * KD, S))
    for h in range(H_RET):
        d = math.log1p(-(2.0 ** (-5.0 - h)))
        n = 1.0 / np.sqrt(np.cumsum(np.exp(d * t)))
        a = np.exp(d * t) * n
        b = np.exp(-d * t) / math.sqrt(KD)
        sl = slice(h * KD, (h + 1) * KD)
        qc[sl] = cos * a[None, :]; qs[sl] = sin * a[None, :]
        kc[sl] = cos * b[None, :]; ks[sl] = sin * b[None, :]
    return qc.astype(BF), qs.astype(BF), kc.astype(BF), ks.astype(BF)


def _ffn_tables():
    t = np.arange(S, dtype=np.float64)
    hi = np.arange(0, DH, 2, dtype=np.float64)
    inv = 1.0 / 10000.0 ** (hi / DH)
    fr = t[:, None] * inv[None, :]
    base = (hi + 0.4 * DH) / (1.4 * DH)
    power = (t - S // 2) / float(WINDOW // 2)
    sc = base[None, :] ** power[:, None]

    def dup2(x):
        y = np.concatenate([x, x], axis=1).T
        return np.concatenate([y, y], axis=0)

    return (dup2(np.cos(fr) * sc).astype(BF), dup2(np.sin(fr) * sc).astype(BF),
            dup2(np.cos(fr) / sc).astype(BF), dup2(np.sin(fr) / sc).astype(BF))


def _consts():
    i = np.arange(128)
    tri_up = (i[None, :] >= i[:, None]).astype(BF)
    tri_lo = (i[None, :] <= i[:, None]).astype(BF)
    p_ret = np.zeros((128, 128), dtype=BF)
    for d in range(0, 128, 2):
        p_ret[d + 1, d] = -1.0; p_ret[d, d + 1] = 1.0
    p_ffn = np.zeros((128, 128), dtype=BF)
    for b0 in (0, 64):
        for d in range(32):
            p_ffn[b0 + d + 32, b0 + d] = -1.0; p_ffn[b0 + d, b0 + d + 32] = 1.0
    sh64 = np.zeros((64, 128), dtype=BF)
    for k in range(64):
        sh64[k, 64 + k] = 1.0
    ind = np.zeros((8, 128, 16), dtype=BF)
    indT = np.zeros((8, 16, 128), dtype=np.float32)
    for fo in range(8):
        for p in range(128):
            slot = (2 * fo + p // 64) if fo < 4 else (8 + 2 * (fo - 4) + p // 64)
            ind[fo, p, slot] = 1.0
            indT[fo, slot, p] = 8.0 if fo < 4 else 1.0
    return dict(
        m_up=tri_up, m_lo=tri_lo, p_ret=p_ret, p_ffn=p_ffn, sh64=sh64,
        id_f32=np.eye(128, dtype=np.float32), id_bf=np.eye(128, dtype=BF),
        ones64f=np.ones((1, 64), dtype=np.float32),
        ones128f=np.ones((1, 128), dtype=np.float32),
        ones128b=np.ones((128, 1), dtype=BF),
        ind=ind, indT32=indT,
    )


class Prog:

    def __init__(self, nc, ctx):
        self.nc = nc
        self.ctx = ctx
        self.eng = {"T": nc.tensor, "V": nc.vector, "S": nc.scalar,
                    "G": nc.gpsimd, "DQ": nc.sync, "GQ": nc.gpsimd}
        self.sem = {k: ctx.enter_context(nc.semaphore(f"sem_{k}"))
                    for k in ("T", "V", "S", "G")}
        self.step = {"T": 1, "V": 1, "S": 1, "G": 1}
        self.cnt = {"T": 0, "V": 0, "S": 0, "G": 0}
        self.waited = {}
        self.bufs = {}
        self.psum = set()
        self.bufsem = {}
        self.nsem = 0
        self.dram_stores = {}

    def _ent(self, b):
        return self.bufs.setdefault(id(b), {"w": None, "r": {}})

    def seed(self, b, readers):
        self.bufs[id(b)] = {"w": None, "r": dict(readers)}

    def _get_bufsem(self, b):
        key = self.bufsem.get(id(b))
        if key is None:
            key = f"b{self.nsem}"
            self.nsem += 1
            self.sem[key] = self.ctx.enter_context(self.nc.semaphore(f"sem_{key}"))
            self.step[key] = 16
            self.cnt[key] = 0
            self.bufsem[id(b)] = key
        return key

    def emit(self, eng, fn, reads=(), writes=(), dma_buf=None, waw_free=False,
             dram_store=False):
        is_dma = eng in ("DQ", "GQ")
        need = {}

        def add(src, cnt, raw):
            old = need.get(src)
            if old is None or cnt > old[0]:
                need[src] = (cnt, raw or (old[1] if old else False))
            elif raw and not old[1]:
                need[src] = (old[0], True)

        for b in reads:
            e = self._ent(b)
            if e["w"] is not None:
                add(e["w"][0], e["w"][1], True)
            if id(b) in self.psum:
                for rk, rv in e["r"].items():
                    add(rk, rv, True)
        for b in writes:
            e = self._ent(b)
            if e["w"] is not None and not waw_free:
                add(e["w"][0], e["w"][1], False)
            for rk, rv in e["r"].items():
                add(rk, rv, False)
        obj = self.eng[eng]
        phys = {"DQ": "DQ", "GQ": "G"}.get(eng, eng)
        for src in sorted(need):
            cnt, raw = need[src]
            if src == "T" and phys == "T":
                continue
            if src == phys and not raw:
                continue
            k = (phys, src)
            if self.waited.get(k, 0) >= cnt:
                continue
            obj.wait_ge(self.sem[src], cnt * self.step[src])
            self.waited[k] = cnt
        ins = fn(obj)
        if isinstance(ins, (list, tuple)):
            ins = ins[-1]
        if is_dma:
            if dma_buf is None:
                dma_buf = writes[0] if writes else reads[0]
            semkey = self._get_bufsem(dma_buf)
        else:
            semkey = eng
        self.cnt[semkey] += 1
        ins.then_inc(self.sem[semkey], self.step[semkey])
        tok = (semkey, self.cnt[semkey])
        if dram_store:
            self.dram_stores[semkey] = self.cnt[semkey]
        for b in reads:
            self._ent(b)["r"][semkey] = self.cnt[semkey]
        for b in writes:
            if waw_free:
                e = self._ent(b)
                e["r"][semkey] = self.cnt[semkey]
            else:
                self.bufs[id(b)] = {"w": tok, "r": {}}
        return tok

    def fence(self):
        for eng in ("T", "V", "S", "G", "DQ"):
            obj = self.eng[eng]
            phys = {"DQ": "DQ"}.get(eng, eng)
            for src in ("T", "V", "S", "G"):
                cnt = self.cnt[src]
                if cnt == 0 or src == phys:
                    continue
                k = (phys, src)
                if self.waited.get(k, 0) >= cnt:
                    continue
                obj.wait_ge(self.sem[src], cnt)
                self.waited[k] = cnt

    def final_wait(self):
        for semkey, cnt in self.dram_stores.items():
            self.nc.sync.wait_ge(self.sem[semkey], cnt * self.step[semkey])


class _View:
    __slots__ = ("t", "bo", "p", "n", "dt", "esz")

    def __init__(self, t, bo, p, n, dt):
        self.t, self.bo, self.p, self.n, self.dt = t, bo, p, n, dt
        self.esz = {F32: 4, BF16: 2, U8: 1}[dt]

    def a(self, p0=0, p1=None, c0=0, c1=None):
        p1 = self.p if p1 is None else p1
        c1 = self.n if c1 is None else c1
        return self.t[p0:p1, self.bo + c0 * self.esz: self.bo + c1 * self.esz].bitcast(self.dt)

    def __getitem__(self, idx):
        ps, cs = idx
        p0 = ps.start or 0
        p1 = self.p if ps.stop is None else ps.stop
        c0 = cs.start or 0
        c1 = self.n if cs.stop is None else cs.stop
        return self.a(p0, p1, c0, c1)

    def ap3(self, coff, dims, p0=0, p1=None):
        p1 = self.p if p1 is None else p1
        base = self.t[p0:p1, self.bo + coff * self.esz: self.bo + (coff + 1) * self.esz].bitcast(self.dt)
        ap = [list(base.ap[0])] + [[s, c] for s, c in dims]
        return bass.AP(base.tensor, base.offset, ap)


def _psap3(ps, coff, dims, p0=0, p1=128):
    base = ps[p0:p1, coff:coff + 1]
    ap = [list(base.ap[0])] + [[s, c] for s, c in dims]
    return bass.AP(base.tensor, base.offset, ap)


class Pool:

    def __init__(self, nc, ctx, prog, nbytes):
        self.t = ctx.enter_context(nc.sbuf_tensor("pool", [128, nbytes], U8))
        self.P = prog
        self.live = []
        self.retired = []
        self.off = 0
        self.nbytes = nbytes

    def reset(self):
        self.retired.extend(self.live)
        self.live = []
        self.off = 0

    def take(self, p, n, dt):
        esz = {F32: 4, BF16: 2, U8: 1}[dt]
        nb = ((n * esz + 3) // 4) * 4
        assert self.off + nb <= self.nbytes, f"pool overflow at {self.off + nb}"
        s, e_ = self.off, self.off + nb
        guard = {}
        keep = []
        for v, vs, ve in self.retired:
            if vs < e_ and s < ve:
                ent = self.P._ent(v)
                if ent["w"] is not None:
                    guard[ent["w"][0]] = max(guard.get(ent["w"][0], 0), ent["w"][1])
                for rk, rv in ent["r"].items():
                    guard[rk] = max(guard.get(rk, 0), rv)
            else:
                keep.append((v, vs, ve))
        self.retired = [(v, vs, ve) for v, vs, ve in self.retired]
        v = _View(self.t, s, p, n, dt)
        self.off = e_
        self.P.seed(v, guard)
        self.live.append((v, s, e_))
        return v


class K:
    pass


def build_program():
    nc = bass.Bass("TRN2", target_bir_lowering=False)
    k = K()
    k.nc = nc
    k.taps = {}

    def din(name, shape, dt=F32):
        return nc.dram_tensor(name, list(shape), dt, kind="ExternalInput")

    k.x_in = din("x", (B, S, D))
    k.W = {n: din(n, s) for n, s in [
        ("wq", (D, D)), ("wk", (D, D)), ("wv", (D, VDIM)), ("wg", (D, VDIM)),
        ("wo", (VDIM, D)), ("wqkv", (2, D, 3 * INNER)), ("wof", (2, INNER, D)),
        ("wa", (2, D, HID)), ("wg1", (2, D, HID)), ("w2", (2, HID, D))]}
    k.qs_c = din("qs_c", (2, 128, 4))
    k.lng_c = din("lng_c", (2, 128, 8)); k.lnb_c = din("lnb_c", (2, 128, 8))
    k.b1a_c = din("b1a_c", (2, 128, NHC)); k.b1g_c = din("b1g_c", (2, 128, NHC))
    k.b2_c = din("b2_c", (2, 128, 8))
    k.t_tab = [din(f"t_tab{i}", (H_RET * KD, S), BF16) for i in range(4)]
    k.f_tab = [din(f"f_tab{i}", (128, S), BF16) for i in range(4)]
    k.CN = {n: din(n, v.shape, {2: BF16, 4: F32}[v.dtype.itemsize])
            for n, v in _consts().items()}
    k.out_d = nc.dram_tensor("out", [B, S, D], F32, kind="ExternalOutput")
    k.WB = {n: nc.dram_tensor(n + "_b", list(k.W[n].shape), BF16, kind="Internal")
            for n in k.W}
    k.WBtok = {n: object() for n in k.W}

    from contextlib import ExitStack
    with ExitStack() as ctx:
        P = Prog(nc, ctx)
        k.P = P

        def sb(name, p, n, dt=F32):
            return ctx.enter_context(nc.sbuf_tensor(name, [p, n], dt))

        k.xf = [sb(f"xf{i}", 128, TT * NTT + 4) for i in range(8)]
        k.acc = [sb(f"acc{i}", 128, S, BF16) for i in range(8)]
        k.stf = [sb(f"st{h}", 128, 257) for h in range(H_RET)]
        k.sthi = sb("sthi", 128, 257, BF16)
        k.stlo = sb("stlo", 128, 257, BF16)
        k.cast = [[sb(f"cast{r}_{i}", 128, TT + 4, BF16) for i in range(8)] for r in range(2)]
        k.wbuf = [sb(f"wbuf{i}", 128, 1024, BF16) for i in range(4)]
        k.wbig = [sb(f"wbig{i}", 128, 2048, BF16) for i in range(2)]
        for n, p, w, dt in [("c_id32", 128, 128, F32), ("c_idbf", 128, 128, BF16),
                            ("c_pret", 128, 128, BF16), ("c_pffn", 128, 128, BF16),
                            ("c_mup", 128, 128, BF16), ("c_mlo", 128, 128, BF16),
                            ("c_sh64", 64, 128, BF16),
                            ("c_o64f", 1, 64, F32), ("c_o128f", 1, 128, F32),
                            ("c_o128b", 128, 1, BF16), ("c_qs", 128, 8, F32),
                            ("c_lng", 128, 16, F32), ("c_lnb", 128, 16, F32),
                            ("c_b1a", 128, 2 * NHC, F32), ("c_b1g", 128, 2 * NHC, F32),
                            ("c_b2", 128, 16, F32), ("eps_t", 128, 16, F32)]:
            setattr(k, n, sb(n, p, w, dt))
        k.c_ind = [sb(f"c_ind{i}", 128, 16, BF16) for i in range(8)]
        k.c_indT = [sb(f"c_indT{i}", 16, 128, F32) for i in range(8)]
        k.PBS = [ctx.enter_context(nc.psum_tensor(f"PB{i}", [128, 512], F32)) for i in range(8)]
        P.psum.update(id(pb) for pb in k.PBS)
        P.psum.update(id(t) for t in k.WBtok.values())
        k.pool = Pool(nc, ctx, P, 61520)

        with nc.Block() as _blk:
            _phase0_consts(k)
            _s0_ln(k, 0)
            _phase0_cast(k, ["wq", "wk", "wv", "wg", "wo"], "DQ")
            if STAGES["ffn"]:
                _phase0_cast(k, ["wqkv", "wof", "wa", "wg1", "w2"], "GQ")
            for b in range(B):
                if b > 0:
                    _s0_ln(k, b)
                if STAGES["attn"]:
                    _retnet(k, b)
                if STAGES["ffn"]:
                    for l in range(2):
                        _ffn_layer(k, b, l)
                        _tap_xf_final(k, b, l)
                _assemble(k, b)
            P.final_wait()
    return k


def _phase0_consts(k):
    P, emit = k.P, k.P.emit
    for name, dst in [("id_f32", k.c_id32), ("id_bf", k.c_idbf), ("p_ret", k.c_pret),
                      ("p_ffn", k.c_pffn), ("m_up", k.c_mup), ("m_lo", k.c_mlo),
                      ("sh64", k.c_sh64), ("ones64f", k.c_o64f),
                      ("ones128f", k.c_o128f), ("ones128b", k.c_o128b)]:
        emit("DQ", lambda e, d=dst, s=k.CN[name]: e.dma_start(d[:, :], s[:, :]), writes=[dst])
    for i in range(8):
        emit("DQ", lambda e, i=i: e.dma_start(k.c_ind[i][:, :], k.CN["ind"][i]),
             writes=[k.c_ind[i]])
        emit("DQ", lambda e, i=i: e.dma_start(k.c_indT[i][:, :], k.CN["indT32"][i]),
             writes=[k.c_indT[i]])
    for l in range(2):
        for dst, src, w in [(k.c_qs, k.qs_c, 4), (k.c_lng, k.lng_c, 8),
                            (k.c_lnb, k.lnb_c, 8), (k.c_b1a, k.b1a_c, NHC),
                            (k.c_b1g, k.b1g_c, NHC), (k.c_b2, k.b2_c, 8)]:
            emit("DQ", lambda e, d=dst, s=src, l=l, w=w:
                 e.dma_start(d[:, l * w:(l + 1) * w], s[l]), writes=[dst])


def _phase0_cast(k, names, queue):
    P, emit = k.P, k.P.emit
    wf32 = [k.pool.take(128, 2048, F32) for _ in range(2)]
    if queue == "DQ":
        stg = k.wbig
    else:
        stg = [k.pool.take(128, 2048, BF16) for _ in range(2)]
    ci = [0]

    def cast2(src, dst, rows, cols, tok):
        for r0 in range(0, rows, 128):
            pr = min(128, rows - r0)
            for c0 in range(0, cols, 2048):
                w = min(2048, cols - c0)
                i = ci[0] % 2; ci[0] += 1
                fb, bb = wf32[i], stg[i]
                emit(queue, lambda e, s=src, r0=r0, pr=pr, c0=c0, w=w, fb=fb:
                     e.dma_start(fb.a(0, pr, 0, w), s[r0:r0 + pr, c0:c0 + w]),
                     writes=[fb])
                emit("V", lambda e, fb=fb, bb=bb, pr=pr, w=w:
                     e.tensor_copy(bb[0:pr, 0:w] if not isinstance(bb, _View) else bb.a(0, pr, 0, w),
                                   fb.a(0, pr, 0, w)),
                     reads=[fb], writes=[bb])
                emit(queue, lambda e, d=dst, r0=r0, pr=pr, c0=c0, w=w, bb=bb:
                     e.dma_start(d[r0:r0 + pr, c0:c0 + w],
                                 bb[0:pr, 0:w] if not isinstance(bb, _View) else bb.a(0, pr, 0, w)),
                     reads=[bb], writes=[tok], dma_buf=bb, waw_free=True)

    for n in names:
        shp = k.W[n].shape
        if len(shp) == 3:
            for l in range(shp[0]):
                cast2(k.W[n][l], k.WB[n][l], shp[1], shp[2], k.WBtok[n])
        else:
            cast2(k.W[n], k.WB[n], shp[0], shp[1], k.WBtok[n])


def _do_cast(k, b, tt, cr):
    emit = k.P.emit
    for fi in range(8):
        emit("V", lambda e, fi=fi, tt=tt, cr=cr: e.tensor_copy(
            k.cast[cr][fi][:, 0:TT + 1], k.xf[fi][:, tt * TT: tt * TT + TT + 1]),
            reads=[k.xf[fi]], writes=[k.cast[cr][fi]])


def _s0_ln(k, b):
    P, emit, pool, PBS = k.P, k.P.emit, k.pool, k.PBS
    ILN = 1.0 / D
    pool.reset()
    tmx = [pool.take(128, D, F32) for _ in range(2)]
    tms = pool.take(128, D, F32)
    lnc = pool.take(128, 8, F32)
    for fc in range(8):
        emit("V", lambda e, fc=fc: e.memset(k.xf[fc][:, 0:1], 0.0), writes=[k.xf[fc]])
    for tc in range(16):
        tm = tmx[tc % 2]
        emit("DQ", lambda e, tm=tm, tc=tc: e.dma_start(
            tm.a(), k.x_in[b, tc * 128:(tc + 1) * 128, :]), writes=[tm])
        emit("V", lambda e, tm=tm: e.reduce_sum(lnc.a(c0=0, c1=1), tm.a(), axis=AX.X),
             reads=[tm], writes=[lnc])
        emit("V", lambda e: e.tensor_scalar_mul(lnc.a(c0=1, c1=2), lnc.a(c0=0, c1=1), -ILN),
             reads=[lnc], writes=[lnc])
        emit("S", lambda e, tm=tm: e.activation(tms.a(), tm.a(), AF.Identity, bias=lnc.a(c0=1, c1=2)),
             reads=[tm, lnc], writes=[tms])
        emit("V", lambda e, tm=tm: e.scalar_tensor_tensor(
            tm.a(), tms.a(), 1.0, tms.a(), ALU.mult, ALU.mult, accum_out=lnc.a(c0=2, c1=3)),
            reads=[tms], writes=[tm, lnc])
        emit("V", lambda e: e.tensor_scalar(lnc.a(c0=3, c1=4), lnc.a(c0=2, c1=3), ILN, EPS,
                                            ALU.mult, ALU.add), reads=[lnc], writes=[lnc])
        emit("S", lambda e: e.activation(lnc.a(c0=4, c1=5), lnc.a(c0=3, c1=4), AF.Ln),
             reads=[lnc], writes=[lnc])
        emit("S", lambda e: e.activation(lnc.a(c0=5, c1=6), lnc.a(c0=4, c1=5), AF.Exp, scale=-0.5),
             reads=[lnc], writes=[lnc])
        emit("V", lambda e: e.tensor_scalar_mul(tms.a(), tms.a(), lnc.a(c0=5, c1=6)),
             reads=[lnc], writes=[tms])
        for fc in range(8):
            pb = PBS[fc % 2]
            emit("T", lambda e, fc=fc, pb=pb: e.transpose(
                pb[:, 0:128], tms.a(c0=fc * 128, c1=(fc + 1) * 128), k.c_id32[:, :]),
                reads=[tms, k.c_id32], writes=[pb])
            if fc % 2 == 0:
                emit("V", lambda e, fc=fc, pb=pb, tc=tc: e.tensor_copy(
                    k.xf[fc][:, 1 + tc * 128: 1 + (tc + 1) * 128], pb[:, 0:128]),
                    reads=[pb], writes=[k.xf[fc]])
            else:
                emit("S", lambda e, fc=fc, pb=pb, tc=tc: e.copy(
                    k.xf[fc][:, 1 + tc * 128: 1 + (tc + 1) * 128], pb[:, 0:128]),
                    reads=[pb], writes=[k.xf[fc]])


def _retnet(k, b):
    P, emit, pool, PBS = k.P, k.P.emit, k.pool, k.PBS
    _tap(k, "xf_pre", k.xf[0][:, 0:513], [k.xf[0]])
    if "wo" not in RET_SUB:
        for fc in range(8):
            emit("V", lambda e, fc=fc: e.memset(k.acc[fc][:, :], 0.0), writes=[k.acc[fc]])
    if "carve" not in RET_SUB:
        return
    pool.reset()
    rtab = [[pool.take(128, TT, BF16) for _ in range(4)] for _ in range(2)]
    r_q = pool.take(128, TT, BF16)
    r_k = pool.take(128, TT, BF16)
    r_rot = pool.take(128, TT, BF16)
    r_t1 = pool.take(128, TT, BF16)
    r_vag = [pool.take(128, 8 * 257 + 2, BF16) for _ in range(4)]
    r_sct = [pool.take(128, TT, BF16) for _ in range(4)]
    r_ktT = pool.take(128, TT, BF16)
    r_hat = [pool.take(128, 256, BF16) for _ in range(2)]
    r_haf = [pool.take(128, TT, BF16) for _ in range(16)]
    r_sil = pool.take(128, TT, BF16)
    r_sq = pool.take(128, 512, BF16)
    wvv = [pool.take(128, 2048, BF16) for _ in range(2)]

    for h in range(H_RET if "stmem" in RET_SUB else 0):
        emit("V", lambda e, h=h: e.memset(k.stf[h][:, 0:257], 0.0), writes=[k.stf[h]])

    for tt in range(NTT if "tt" in RET_SUB else 0):
        cr = tt % 2
        if "cast" in RET_SUB:
            _do_cast(k, b, tt, cr)
        cc = k.cast[cr]

        for vc in range(8 if 'vproj' in RET_SUB else 0):
            wv = wvv[vc % 2]
            emit("DQ", lambda e, vc=vc, wv=wv: e.dma_start(
                wv.ap3(0, [[256, 8], [1, 256]]),
                bass.AP(k.WB["wv"], vc * 256, [[VDIM, 128], [128 * VDIM, 8], [1, 256]])),
                reads=[k.WBtok["wv"]], writes=[wv])
            for jt in range(4):
                pv = PBS[4 + (jt % 2)]
                for fi in range(8):
                    emit("T", lambda e, fi=fi, jt=jt, wv=wv, pv=pv: e.matmul(
                        pv[:, 0:256], cc[fi][:, 1 + jt * 128: 1 + (jt + 1) * 128],
                        wv.a(c0=fi * 256, c1=(fi + 1) * 256),
                        start=(fi == 0), stop=(fi == 7)),
                        reads=[cc[fi], wv], writes=[pv])
                emit("V", lambda e, jt=jt, vc=vc, pv=pv: e.tensor_copy(
                    r_vag[jt].a(c0=257 * vc, c1=257 * vc + 256), pv[:, 0:256]),
                    reads=[pv], writes=[r_vag[jt]])
        for jt in range(4 if 'vproj' in RET_SUB else 0):
            emit("V", lambda e, jt=jt: e.memset(
                r_vag[jt].ap3(256, [[257, 8], [1, 1]]), 1.0),
                reads=[r_vag[jt]], writes=[r_vag[jt]])

        for h in range(H_RET if RET_SUB & {'tables', 'qkproj'} else 0):
            ts = rtab[h % 2]
            for i in range(4 if 'tables' in RET_SUB else 0):
                emit("DQ", lambda e, i=i, h=h, tt=tt: e.dma_start(
                    ts[i].a(), k.t_tab[i][h * 128:(h + 1) * 128, tt * TT:(tt + 1) * TT]),
                    writes=[ts[i]])
            if 'qkproj' not in RET_SUB:
                continue
            wq = k.wbuf[0 + (h % 2)]
            wk = k.wbuf[2 + (h % 2)]
            emit("DQ", lambda e, h=h, wq=wq: e.dma_start(
                wq[:, 0:1024].rearrange("p (c n) -> p c n", c=8),
                bass.AP(k.WB["wq"], h * 128, [[D, 128], [128 * D, 8], [1, 128]])),
                reads=[k.WBtok["wq"]], writes=[wq])
            emit("DQ", lambda e, h=h, wk=wk: e.dma_start(
                wk[:, 0:1024].rearrange("p (c n) -> p c n", c=8),
                bass.AP(k.WB["wk"], h * 128, [[D, 128], [128 * D, 8], [1, 128]])),
                reads=[k.WBtok["wk"]], writes=[wk])
            pq, pk, pro = PBS[0], PBS[1], PBS[2]
            for fi in range(8):
                emit("T", lambda e, fi=fi, wq=wq, pq=pq: e.matmul(
                    pq[:, 0:512], wq[:, fi * 128:(fi + 1) * 128], cc[fi][:, 1:1 + TT],
                    start=(fi == 0), stop=(fi == 7)),
                    reads=[wq, cc[fi]], writes=[pq])
            for fi in range(8):
                emit("T", lambda e, fi=fi, wk=wk, pk=pk: e.matmul(
                    pk[:, 0:512], wk[:, fi * 128:(fi + 1) * 128], cc[fi][:, 1:1 + TT],
                    start=(fi == 0), stop=(fi == 7)),
                    reads=[wk, cc[fi]], writes=[pk])

            def rotary(psrc, tcs, tsn, dst):
                emit("V", lambda e: e.tensor_copy(r_rot[:, 0:TT], psrc[:, 0:512]),
                     reads=[psrc], writes=[r_rot])
                emit("T", lambda e: e.matmul(pro[:, 0:512], k.c_pret[:, :], r_rot[:, 0:TT],
                                             start=True, stop=True),
                     reads=[k.c_pret, r_rot], writes=[pro])
                emit("V", lambda e: e.scalar_tensor_tensor(
                    r_t1[:, 0:TT], psrc[:, 0:512], 1.0, tcs.a(), ALU.mult, ALU.mult),
                    reads=[psrc, tcs], writes=[r_t1])
                emit("V", lambda e: e.scalar_tensor_tensor(
                    r_rot[:, 0:TT], pro[:, 0:512], 1.0, tsn.a(), ALU.mult, ALU.mult),
                    reads=[pro, tsn], writes=[r_rot])
                emit("V", lambda e: e.tensor_add(dst[:, 0:TT], r_rot[:, 0:TT], r_t1[:, 0:TT]),
                     reads=[r_t1, r_rot], writes=[dst])

            if 'rot' not in RET_SUB:
                continue
            rotary(pq, ts[0], ts[1], r_q)
            rotary(pk, ts[2], ts[3], r_k)
            if tt == 0 and h == 0 and b == 0:
                _tap(k, "cast0", cc[0][:, 0:TT + 1], [cc[0]])
                _tap(k, "rq", r_q.a(), [r_q])
                _tap(k, "rk", r_k.a(), [r_k])

            for jc in range(4 if 'scores' in RET_SUB else 0):
                psc = PBS[2 + (jc % 2)]
                emit("T", lambda e, jc=jc, psc=psc: e.matmul(
                    psc[:, jc * 128:512], r_k[:, jc * 128:(jc + 1) * 128],
                    r_q[:, jc * 128:TT], start=True, stop=True),
                    reads=[r_k, r_q], writes=[psc])
                emit("V", lambda e, jc=jc, psc=psc: e.tensor_mul(
                    r_sct[jc][:, jc * 128:(jc + 1) * 128],
                    psc[:, jc * 128:(jc + 1) * 128], k.c_mup[:, :]),
                    reads=[psc, k.c_mup], writes=[r_sct[jc]])
                if jc < 3:
                    emit("S", lambda e, jc=jc, psc=psc: e.copy(
                        r_sct[jc][:, (jc + 1) * 128:512], psc[:, (jc + 1) * 128:512]),
                        reads=[psc], writes=[r_sct[jc]])

            if tt == 0 and h == 0 and b == 0:
                _tap(k, "sct0", r_sct[0].a(), [r_sct[0]])
                _tap(k, "sct1", r_sct[1].a(), [r_sct[1]])
                _tap(k, "vag0", r_vag[0].a(), [r_vag[0]])
            pkt = PBS[2]
            for jc in range(4 if 'ktt' in RET_SUB else 0):
                emit("T", lambda e, jc=jc, pkt=pkt: e.matmul(
                    pkt[:, jc * 128:(jc + 1) * 128], r_k[:, jc * 128:(jc + 1) * 128],
                    k.c_idbf[:, :], start=True, stop=True),
                    reads=[r_k, k.c_idbf], writes=[pkt])
            if 'ktt' in RET_SUB:
                emit("V", lambda e, pkt=pkt: e.tensor_copy(r_ktT[:, 0:512], pkt[:, 0:512]),
                     reads=[pkt], writes=[r_ktT])

            if 'value' not in RET_SUB:
                continue
            emit("V", lambda e, h=h: e.tensor_copy(k.sthi[:, 0:257], k.stf[h][:, 0:257]),
                 reads=[k.stf[h]], writes=[k.sthi])
            emit("V", lambda e, h=h: e.tensor_sub(k.stlo[:, 0:257], k.stf[h][:, 0:257],
                                                  k.sthi[:, 0:257]),
                 reads=[k.stf[h], k.sthi], writes=[k.stlo])

            for qc in range(4):
                po = PBS[6 + (qc % 2)]
                for jc in range(qc + 1):
                    emit("T", lambda e, jc=jc, qc=qc, po=po, h=h: e.matmul(
                        po[:, 0:257], r_sct[jc][:, qc * 128:(qc + 1) * 128],
                        r_vag[jc].a(c0=257 * h, c1=257 * h + 257),
                        start=(jc == 0), stop=False),
                        reads=[r_sct[jc], r_vag[jc]], writes=[po])
                emit("T", lambda e, qc=qc, po=po: e.matmul(
                    po[:, 0:257], r_q[:, qc * 128:(qc + 1) * 128], k.sthi[:, 0:257],
                    start=False, stop=False),
                    reads=[r_q, k.sthi], writes=[po])
                emit("T", lambda e, qc=qc, po=po: e.matmul(
                    po[:, 0:257], r_q[:, qc * 128:(qc + 1) * 128], k.stlo[:, 0:257],
                    start=False, stop=True),
                    reads=[r_q, k.stlo], writes=[po])
                et = k.eps_t
                emit("V", lambda e, po=po: e.tensor_scalar_mul(
                    et[:, 10:11], po[:, 256:257], -1.0),
                    reads=[po], writes=[et])
                emit("V", lambda e, po=po: e.scalar_tensor_tensor(
                    et[:, 0:1], et[:, 10:11], 1.0, po[:, 256:257],
                    ALU.max, ALU.max),
                    reads=[po, et], writes=[et])
                emit("V", lambda e: e.reciprocal(et[:, 1:2], et[:, 0:1]),
                     reads=[et], writes=[et])
                emit("S", lambda e, po=po: e.activation(
                    r_sq[:, 0:256], po[:, 0:256], AF.Square, accum_out=et[:, 2:3]),
                    reads=[po], writes=[r_sq, et])
                emit("V", lambda e: e.tensor_scalar(
                    et[:, 3:4], et[:, 2:3], 1.0 / 256.0, None, ALU.mult),
                    reads=[et], writes=[et])
                emit("V", lambda e: e.tensor_mul(et[:, 4:5], et[:, 3:4], et[:, 1:2]),
                     reads=[et], writes=[et])
                emit("V", lambda e: e.tensor_mul(et[:, 5:6], et[:, 4:5], et[:, 1:2]),
                     reads=[et], writes=[et])
                emit("V", lambda e: e.tensor_scalar(
                    et[:, 6:7], et[:, 5:6], 1.0, EPS, ALU.mult, ALU.add),
                    reads=[et], writes=[et])
                emit("S", lambda e: e.activation(et[:, 7:8], et[:, 6:7], AF.Ln),
                     reads=[et], writes=[et])
                emit("S", lambda e: e.activation(et[:, 8:9], et[:, 7:8], AF.Exp, scale=-0.5),
                     reads=[et], writes=[et])
                emit("V", lambda e: e.tensor_mul(et[:, 9:10], et[:, 8:9], et[:, 1:2]),
                     reads=[et], writes=[et])
                hb = r_hat[qc % 2]
                emit("V", lambda e, po=po, hb=hb: e.tensor_scalar(
                    hb.a(), po[:, 0:256], et[:, 9:10], None, ALU.mult),
                    reads=[po, et], writes=[hb])
                for blk in range(2):
                    pt = PBS[4 + blk]
                    emit("T", lambda e, blk=blk, qc=qc, hb=hb, pt=pt: e.matmul(
                        pt[:, qc * 128:(qc + 1) * 128],
                        hb.a(c0=blk * 128, c1=(blk + 1) * 128), k.c_idbf[:, :],
                        start=True, stop=True),
                        reads=[hb, k.c_idbf], writes=[pt])
            for blk in range(2):
                pt = PBS[4 + blk]
                emit("V", lambda e, blk=blk, pt=pt, h=h: e.tensor_copy(
                    r_haf[2 * h + blk].a(), pt[:, 0:512]),
                    reads=[pt], writes=[r_haf[2 * h + blk]])
            if tt == 0 and h == 0 and b == 0:
                _tap(k, "haf0", r_haf[0].a(), [r_haf[0]])
                _tap(k, "ktT", r_ktT.a(), [r_ktT])

            pkv = PBS[5]
            for jc in range(4 if 'state' in RET_SUB else 0):
                emit("T", lambda e, jc=jc, h=h, pkv=pkv: e.matmul(
                    pkv[:, 0:257], r_ktT[:, jc * 128:(jc + 1) * 128],
                    r_vag[jc].a(c0=257 * h, c1=257 * h + 257),
                    start=(jc == 0), stop=(jc == 3)),
                    reads=[r_ktT, r_vag[jc]], writes=[pkv])
            if 'state' in RET_SUB:
                emit("V", lambda e, h=h, pkv=pkv: e.tensor_add(
                    k.stf[h][:, 0:257], k.stf[h][:, 0:257], pkv[:, 0:257]),
                    reads=[pkv, k.stf[h]], writes=[k.stf[h]])
            if tt == 0 and h == 0 and b == 0:
                _tap(k, "st0", k.stf[0][:, 0:257], [k.stf[0]])

        for vd in range(16 if 'silu' in RET_SUB else 0):
            wgv = k.wbuf[vd % 2]
            emit("DQ", lambda e, vd=vd, wgv=wgv: e.dma_start(
                wgv[:, 0:1024].rearrange("p (c n) -> p c n", c=8),
                bass.AP(k.WB["wg"], vd * 128, [[VDIM, 128], [128 * VDIM, 8], [1, 128]])),
                reads=[k.WBtok["wg"]], writes=[wgv])
            pg = PBS[0 + (vd % 2)]
            for fi in range(8):
                emit("T", lambda e, fi=fi, wgv=wgv, pg=pg: e.matmul(
                    pg[:, 0:512], wgv[:, fi * 128:(fi + 1) * 128], cc[fi][:, 1:1 + TT],
                    start=(fi == 0), stop=(fi == 7)),
                    reads=[wgv, cc[fi]], writes=[pg])
            emit("S", lambda e, pg=pg: e.activation(r_sil[:, 0:TT], pg[:, 0:512], AF.Silu),
                 reads=[pg], writes=[r_sil])
            emit("V", lambda e, vd=vd: e.tensor_mul(
                r_haf[vd].a(), r_haf[vd].a(), r_sil[:, 0:TT]),
                reads=[r_sil, r_haf[vd]], writes=[r_haf[vd]])


        for fc in range(8 if 'wo' in RET_SUB else 0):
            wov = k.wbig[fc % 2]
            emit("DQ", lambda e, fc=fc, wov=wov: e.dma_start(
                wov[:, 0:2048].rearrange("p (c n) -> p c n", c=16),
                bass.AP(k.WB["wo"], fc * 128, [[D, 128], [128 * D, 16], [1, 128]])),
                reads=[k.WBtok["wo"]], writes=[wov])
            pw = PBS[2 + (fc % 2)]
            for vd in range(16):
                emit("T", lambda e, vd=vd, wov=wov, pw=pw: e.matmul(
                    pw[:, 0:512], wov[:, vd * 128:(vd + 1) * 128], r_haf[vd].a(),
                    start=(vd == 0), stop=(vd == 15)),
                    reads=[wov, r_haf[vd]], writes=[pw])
            emit("V", lambda e, fc=fc, tt=tt, pw=pw: e.tensor_copy(
                k.acc[fc][:, tt * TT:(tt + 1) * TT], pw[:, 0:512]),
                reads=[pw], writes=[k.acc[fc]])
        if tt == 0 and b == 0:
            _tap(k, "acc0", k.acc[0][:, 0:TT], [k.acc[0]])
            _tap(k, "sil15", r_sil.a(), [r_sil])


    _tap(k, "xf_post", k.xf[0][:, 0:513], [k.xf[0]])
    _tap(k, "stf7", k.stf[7][:, 0:257], [k.stf[7]])


def _ffn_layer(k, b, l):
    P, emit, pool, PBS = k.P, k.P.emit, k.pool, k.PBS
    pool.reset()
    f_qr = [pool.take(128, TT, BF16) for _ in range(4)]
    f_kr = [pool.take(128, 2 * TT, BF16) for _ in range(4)]
    f_vag = [pool.take(128, 8 * 65 + 2, BF16) for _ in range(8)]
    f_pt = [pool.take(128, TT, BF16) for _ in range(2)]
    f_o = [pool.take(128, TT, BF16) for _ in range(4)]
    f_sig = pool.take(16, TT, F32)
    f_row = [pool.take(1, TT, F32) for _ in range(2)]
    f_rot = pool.take(128, TT, BF16)
    f_t1 = pool.take(128, TT, BF16)
    f_sq = pool.take(128, TT, BF16)
    f_bc = pool.take(64, TT, BF16)
    f_ot = pool.take(64, TT, BF16)
    wvv = [pool.take(128, 2048, BF16) for _ in range(2)]
    tabt = pool.take(128, TT, BF16)
    ffold = [pool.take(128, TT, BF16) for _ in range(4)]

    for tt in range(NTT):
        cr = tt % 2
        if tt == 0:
            _do_cast(k, b, 0, 0)
        cc = k.cast[cr]

        for i in range(4 if 'fold' in FFN_SUB else 0):
            emit("DQ", lambda e, i=i, tt=tt: e.dma_start(
                tabt.a(), k.f_tab[i][:, tt * TT:(tt + 1) * TT]),
                writes=[tabt])
            emit("V", lambda e, i=i: e.tensor_scalar_mul(
                ffold[i].a(), tabt.a(), k.c_qs[:, l * 4 + i: l * 4 + i + 1]),
                reads=[tabt, k.c_qs], writes=[ffold[i]])

        if 'vproj' in FFN_SUB:
            for vh in range(2):
                wvp = wvv[vh]
                emit("DQ", lambda e, vh=vh, wvp=wvp: e.dma_start(
                    wvp.ap3(0, [[256, 8], [1, 256]]),
                    bass.AP(k.WB["wqkv"], l * D * 3 * INNER + 2 * INNER + vh * 256,
                            [[3 * INNER, 128], [128 * 3 * INNER, 8], [1, 256]])),
                    reads=[k.WBtok["wqkv"]], writes=[wvp])
                for jt in range(4):
                    slot = (tt * 4 + jt) % 8
                    pv = PBS[4 + (jt % 2)]
                    for fi in range(8):
                        sh = 0 if fi < 4 else -1
                        emit("T", lambda e, fi=fi, jt=jt, sh=sh, pv=pv, wvp=wvp: e.matmul(
                            pv[:, 0:256], cc[fi][:, 1 + sh + jt * 128: 1 + sh + (jt + 1) * 128],
                            wvp.a(c0=fi * 256, c1=(fi + 1) * 256),
                            start=(fi == 0), stop=(fi == 7)),
                            reads=[cc[fi], wvp], writes=[pv])
                    emit("V", lambda e, slot=slot, vh=vh, pv=pv: e.tensor_copy(
                        f_vag[slot].ap3(65 * 4 * vh, [[65, 4], [1, 64]]),
                        _psap3(pv, 0, [[64, 4], [1, 64]])),
                        reads=[pv], writes=[f_vag[slot]])
            for jt in range(4):
                slot = (tt * 4 + jt) % 8
                emit("V", lambda e, slot=slot: e.memset(
                    f_vag[slot].ap3(64, [[65, 8], [1, 1]]), 1.0),
                    reads=[f_vag[slot]], writes=[f_vag[slot]])

        pn = PBS[3]
        for fo in range(8 if 'qk' in FFN_SUB else 0):
            wql = k.wbuf[fo % 2]
            emit("DQ", lambda e, fo=fo, wql=wql: e.dma_start(
                wql[:, 0:1024].rearrange("p (c n) -> p c n", c=8),
                bass.AP(k.WB["wqkv"], l * D * 3 * INNER + fo * 128,
                        [[3 * INNER, 128], [128 * 3 * INNER, 8], [1, 128]])),
                reads=[k.WBtok["wqkv"]], writes=[wql])
            pq = PBS[0 + (fo % 2)]
            for fi in range(8):
                sh = 0 if fi < 4 else -1
                emit("T", lambda e, fi=fi, sh=sh, wql=wql, pq=pq: e.matmul(
                    pq[:, 0:512], wql[:, fi * 128:(fi + 1) * 128],
                    cc[fi][:, 1 + sh: 1 + sh + TT],
                    start=(fi == 0), stop=(fi == 7)),
                    reads=[wql, cc[fi]], writes=[pq])
            if 'qknorm' in FFN_SUB:
                emit("S", lambda e, pq=pq: e.activation(f_sq.a(), pq[:, 0:512], AF.Square),
                     reads=[pq], writes=[f_sq])
                emit("T", lambda e, fo=fo, pn=pn: e.matmul(
                    pn[0:16, 0:512], k.c_ind[fo][:, :], f_sq.a(),
                    start=(fo == 0), stop=(fo == 7)),
                    reads=[k.c_ind[fo], f_sq], writes=[pn])
            if 'qkrot' not in FFN_SUB:
                continue
            tcs = ffold[0] if fo < 4 else ffold[2]
            tsn = ffold[1] if fo < 4 else ffold[3]
            pro = PBS[2]
            emit("V", lambda e, pq=pq: e.tensor_copy(f_rot.a(), pq[:, 0:512]),
                 reads=[pq], writes=[f_rot])
            emit("T", lambda e, pro=pro: e.matmul(pro[:, 0:512], k.c_pffn[:, :], f_rot.a(),
                                                  start=True, stop=True),
                 reads=[k.c_pffn, f_rot], writes=[pro])
            emit("V", lambda e, pq=pq, tcs=tcs: e.scalar_tensor_tensor(
                f_t1.a(), pq[:, 0:512], 1.0, tcs.a(), ALU.mult, ALU.mult),
                reads=[pq, tcs], writes=[f_t1])
            emit("V", lambda e, pro=pro, tsn=tsn: e.scalar_tensor_tensor(
                f_rot.a(), pro[:, 0:512], 1.0, tsn.a(), ALU.mult, ALU.mult),
                reads=[pro, tsn], writes=[f_rot])
            dst = f_qr[fo] if fo < 4 else f_kr[fo - 4]
            co = 0 if fo < 4 else (tt % 2) * TT
            emit("V", lambda e, dst=dst, co=co: e.tensor_add(
                dst.a(c0=co, c1=co + TT), f_rot.a(), f_t1.a()),
                reads=[f_rot, f_t1], writes=[dst])
        if 'qknorm' not in FFN_SUB:
            continue
        emit("V", lambda e, pn=pn: e.tensor_scalar_max(f_sig.a(0, 16), pn[0:16, 0:512], 1e-24),
             reads=[pn], writes=[f_sig])
        emit("S", lambda e: e.activation(f_sig.a(0, 16), f_sig.a(0, 16), AF.Ln),
             reads=[f_sig], writes=[f_sig])
        emit("S", lambda e: e.activation(f_sig.a(0, 16), f_sig.a(0, 16), AF.Exp, scale=-0.5),
             reads=[f_sig], writes=[f_sig])
        for fo in range(8 if 'sigb' in FFN_SUB else 0):
            pb2 = PBS[2]
            emit("T", lambda e, fo=fo, pb2=pb2: e.matmul(
                pb2[:, 0:512], k.c_indT[fo][:, :], f_sig.a(0, 16), start=True, stop=True),
                reads=[k.c_indT[fo], f_sig], writes=[pb2])
            dst = f_qr[fo] if fo < 4 else f_kr[fo - 4]
            co = 0 if fo < 4 else (tt % 2) * TT
            emit("V", lambda e, dst=dst, co=co, pb2=pb2: e.scalar_tensor_tensor(
                dst.a(c0=co, c1=co + TT), pb2[:, 0:512], 1.0, dst.a(c0=co, c1=co + TT),
                ALU.mult, ALU.mult),
                reads=[pb2, dst], writes=[dst])

        if b == 0 and l == 0 and tt == 0:
            _tap(k, "fqr0", f_qr[0].a(), [f_qr[0]])
            _tap(k, "fkr0", f_kr[0].a(c0=0, c1=TT), [f_kr[0]])
            _tap(k, "fvag0", f_vag[0].a(), [f_vag[0]])
            _tap(k, "fsig0", f_sig.a(0, 16), [f_sig])
        if tt + 1 < NTT:
            _do_cast(k, b, tt + 1, (tt + 1) % 2)

        def do_div(h, po):
            fr = f_row[h % 2]
            pb2 = PBS[0]
            emit("T", lambda e, pb2=pb2, fr=fr: e.matmul(
                pb2[0:64, 0:512], k.c_o64f[:, :], fr.a(0, 1),
                start=True, stop=True),
                reads=[k.c_o64f, fr], writes=[pb2])
            emit("V", lambda e, pb2=pb2: e.tensor_copy(f_bc.a(0, 64), pb2[0:64, 0:512]),
                 reads=[pb2], writes=[f_bc])
            if h % 2 == 0:
                emit("V", lambda e, h=h, po=po: e.tensor_mul(
                    f_o[h // 2].a(p0=0, p1=64), po[0:64, 0:512], f_bc.a(0, 64)),
                    reads=[po, f_bc], writes=[f_o[h // 2]])
            else:
                emit("V", lambda e, po=po: e.tensor_mul(
                    f_ot.a(0, 64), po[0:64, 0:512], f_bc.a(0, 64)),
                    reads=[po, f_bc], writes=[f_ot])
                psh = PBS[4]
                emit("T", lambda e, psh=psh: e.matmul(
                    psh[:, 0:512], k.c_sh64[:, :], f_ot.a(0, 64), start=True, stop=True),
                    reads=[k.c_sh64, f_ot], writes=[psh])
                emit("V", lambda e, h=h, psh=psh: e.tensor_copy(
                    f_o[h // 2].a(p0=64, p1=128), psh[64:128, 0:512]),
                    reads=[psh], writes=[f_o[h // 2]])

        div_q = []
        for h in range(H_RET if 'attn' in FFN_SUB else 0):
            fo2 = h // 2
            pb_ = 64 * (h % 2)
            po = PBS[5 + (h % 3)]
            first = True
            stages = []
            for i in range(8):
                if tt == 0 and i < 4:
                    continue
                tile = tt - 1 + (i // 4)
                kco = (tile % 2) * TT + (i % 4) * 128
                if i < 4:
                    q0, q1 = 0, 128 * (i + 1)
                else:
                    q0, q1 = 128 * (i - 4), 512
                stages.append((i, tile, kco, q0, q1))

            def do_sim(st):
                i, tile, kco, q0, q1 = st
                psim = PBS[2 + (i % 2)]
                emit("T", lambda e, fo2=fo2, pb_=pb_, kco=kco, q0=q0, q1=q1, psim=psim: e.matmul(
                    psim[:, q0:q1],
                    f_kr[fo2].a(p0=pb_, p1=pb_ + 64, c0=kco, c1=kco + 128),
                    f_qr[fo2].a(p0=pb_, p1=pb_ + 64, c0=q0, c1=q1),
                    start=True, stop=True),
                    reads=[f_kr[fo2], f_qr[fo2]], writes=[psim])

            def do_rest(st, first):
                i, tile, kco, q0, q1 = st
                psim = PBS[2 + (i % 2)]
                pt = f_pt[i % 2]
                emit("S", lambda e, q0=q0, q1=q1, psim=psim, pt=pt: e.activation(
                    pt.a(c0=q0, c1=q1), psim[:, q0:q1], AF.Exp),
                    reads=[psim], writes=[pt])
                if i < 4:
                    mb = 128 * i
                    emit("V", lambda e, mb=mb, pt=pt: e.tensor_mul(
                        pt.a(c0=mb, c1=mb + 128), pt.a(c0=mb, c1=mb + 128), k.c_mlo[:, :]),
                        reads=[k.c_mlo, pt], writes=[pt])
                else:
                    mb = 128 * (i - 4)
                    emit("V", lambda e, mb=mb, pt=pt: e.tensor_mul(
                        pt.a(c0=mb, c1=mb + 128), pt.a(c0=mb, c1=mb + 128), k.c_mup[:, :]),
                        reads=[k.c_mup, pt], writes=[pt])
                slot = (tile * 4 + (i % 4)) % 8
                emit("T", lambda e, slot=slot, h=h, q0=q0, q1=q1, po=po, pt=pt, first=first:
                     e.matmul(po[0:65, q0:q1], f_vag[slot].a(c0=65 * h, c1=65 * h + 65),
                              pt.a(c0=q0, c1=q1), start=first, stop=False),
                     reads=[f_vag[slot], pt], writes=[po])

            prev = None
            for st in stages:
                do_sim(st)
                if prev is not None:
                    do_rest(prev, first)
                    first = False
                prev = st
            do_rest(prev, first)
            first = False
            fr = f_row[h % 2]
            emit("V", lambda e, po=po, fr=fr: e.reciprocal(
                fr.a(0, 1), po[64:65, 0:512]),
                reads=[po], writes=[fr])
            div_q.append((h, po))
            if len(div_q) > 1:
                do_div(*div_q.pop(0))
        while div_q:
            do_div(*div_q.pop(0))

        if b == 0 and l == 0 and tt == 0:
            _tap(k, "fo0", f_o[0].a(), [f_o[0]])
        for fc in range(8 if 'wo' in FFN_SUB else 0):
            wofv = k.wbuf[2 + (fc % 2)]
            emit("DQ", lambda e, fc=fc, wofv=wofv: e.dma_start(
                wofv[:, 0:512].rearrange("p (c n) -> p c n", c=4),
                bass.AP(k.WB["wof"], l * INNER * D + fc * 128,
                        [[D, 128], [128 * D, 4], [1, 128]])),
                reads=[k.WBtok["wof"]], writes=[wofv])
            pw = PBS[0 + (fc % 2)]
            for ic in range(4):
                emit("T", lambda e, ic=ic, wofv=wofv, pw=pw: e.matmul(
                    pw[:, 0:512], wofv[:, ic * 128:(ic + 1) * 128], f_o[ic].a(),
                    start=(ic == 0), stop=(ic == 3)),
                    reads=[wofv, f_o[ic]], writes=[pw])
            emit("V", lambda e, fc=fc, tt=tt, pw=pw: e.scalar_tensor_tensor(
                k.xf[fc][:, 1 + tt * TT: 1 + (tt + 1) * TT], pw[:, 0:512], 1.0,
                k.xf[fc][:, 1 + tt * TT: 1 + (tt + 1) * TT], ALU.mult, ALU.add),
                reads=[pw, k.xf[fc]], writes=[k.xf[fc]])

    if b == 0 and l == 0:
        for fi in range(8):
            _tap(k, f"xf_mid{fi}", k.xf[fi][:, 0:2052], [k.xf[fi]])
    pool.reset()
    f_hln = [pool.take(128, TT, BF16) for _ in range(8)]
    f_ag = [pool.take(128, TT, BF16) for _ in range(NHC)]
    f_gel = pool.take(128, TT, BF16)
    f_sq2 = pool.take(128, TT, BF16)
    f_tmp = pool.take(128, TT, BF16)
    rowA = pool.take(1, TT, F32)
    rowC = pool.take(1, TT, F32)
    rowV = pool.take(1, TT, F32)
    rowM = pool.take(1, TT, F32)
    w2v = [pool.take(128, 1024, BF16) for _ in range(2)]

    for tq in (reversed(range(NTT)) if FFN_SUB & {'ln', 'w1', 'w2'} else []):
        cr = tq % 2
        _do_cast(k, b, tq, cr)
        cc = k.cast[cr]
        pmu, pms = PBS[6], PBS[7]
        for fi in range(8):
            sh = 0 if fi < 4 else -1
            emit("T", lambda e, fi=fi, sh=sh, pmu=pmu: e.matmul(
                pmu[0:1, 0:512], k.c_o128b[:, :], cc[fi][:, 1 + sh: 1 + sh + TT],
                start=(fi == 0), stop=(fi == 7)),
                reads=[k.c_o128b, cc[fi]], writes=[pmu])
        for fi in range(8):
            sh = 0 if fi < 4 else -1
            emit("S", lambda e, fi=fi, sh=sh: e.activation(
                f_sq2.a(), cc[fi][:, 1 + sh: 1 + sh + TT], AF.Square),
                reads=[cc[fi]], writes=[f_sq2])
            emit("T", lambda e, fi=fi, pms=pms: e.matmul(
                pms[0:1, 0:512], k.c_o128b[:, :], f_sq2.a(),
                start=(fi == 0), stop=(fi == 7)),
                reads=[k.c_o128b, f_sq2], writes=[pms])
        emit("V", lambda e, pmu=pmu: e.tensor_scalar_mul(rowA.a(0, 1), pmu[0:1, 0:512], 1.0 / D),
             reads=[pmu], writes=[rowA])
        emit("V", lambda e: e.tensor_mul(rowC.a(0, 1), rowA.a(0, 1), rowA.a(0, 1)),
             reads=[rowA], writes=[rowC])
        emit("V", lambda e, pms=pms: e.scalar_tensor_tensor(
            rowV.a(0, 1), pms[0:1, 0:512], 1.0 / D, rowC.a(0, 1), ALU.mult, ALU.subtract),
            reads=[pms, rowC], writes=[rowV])
        emit("V", lambda e: e.tensor_scalar_add(rowV.a(0, 1), rowV.a(0, 1), EPS),
             reads=[rowV], writes=[rowV])
        emit("S", lambda e: e.activation(rowV.a(0, 1), rowV.a(0, 1), AF.Ln),
             reads=[rowV], writes=[rowV])
        emit("S", lambda e: e.activation(rowV.a(0, 1), rowV.a(0, 1), AF.Exp, scale=-0.5),
             reads=[rowV], writes=[rowV])
        emit("V", lambda e: e.tensor_mul(rowM.a(0, 1), rowA.a(0, 1), rowV.a(0, 1)),
             reads=[rowA, rowV], writes=[rowM])
        pbR, pbM = PBS[4], PBS[5]
        emit("T", lambda e, pbR=pbR: e.matmul(
            pbR[:, 0:512], k.c_o128f[:, :], rowV.a(0, 1), start=True, stop=True),
            reads=[k.c_o128f, rowV], writes=[pbR])
        emit("T", lambda e, pbM=pbM: e.matmul(
            pbM[:, 0:512], k.c_o128f[:, :], rowM.a(0, 1), start=True, stop=True),
            reads=[k.c_o128f, rowM], writes=[pbM])
        for fi in range(8):
            sh = 0 if fi < 4 else -1
            emit("V", lambda e, fi=fi, sh=sh, pbR=pbR: e.scalar_tensor_tensor(
                f_tmp.a(), cc[fi][:, 1 + sh: 1 + sh + TT], 1.0, pbR[:, 0:512],
                ALU.mult, ALU.mult),
                reads=[cc[fi], pbR], writes=[f_tmp])
            emit("V", lambda e, pbM=pbM: e.tensor_sub(f_tmp.a(), f_tmp.a(), pbM[:, 0:512]),
                 reads=[pbM, f_tmp], writes=[f_tmp])
            emit("S", lambda e, fi=fi: e.activation(
                f_hln[fi].a(), f_tmp.a(), AF.Identity,
                scale=k.c_lng[:, l * 8 + fi: l * 8 + fi + 1],
                bias=k.c_lnb[:, l * 8 + fi: l * 8 + fi + 1]),
                reads=[f_tmp, k.c_lng, k.c_lnb], writes=[f_hln[fi]])

        if b == 0 and l == 0 and tq == 3:
            _tap(k, "fhln0", f_hln[0].a(), [f_hln[0]])
        if b == 0 and l == 0 and tq == 0:
            _tap(k, "fhln0_t0", f_hln[0].a(), [f_hln[0]])
            _tap(k, "fhln7_t0", f_hln[7].a(), [f_hln[7]])
        for hp in range(NHC if 'w1' in FFN_SUB else 0):
            hw = 128 if hp < NHC - 1 else HID - 128 * (NHC - 1)
            wav, wgv = k.wbuf[0 + (hp % 2)], k.wbuf[2 + (hp % 2)]
            for wsb, wsrc in ((wav, k.WB["wa"]), (wgv, k.WB["wg1"])):
                emit("DQ", lambda e, hp=hp, hw=hw, wsb=wsb, wsrc=wsrc: e.dma_start(
                    wsb[:, 0:8 * hw].rearrange("p (c n) -> p c n", c=8),
                    bass.AP(wsrc, l * D * HID + hp * 128,
                            [[HID, 128], [128 * HID, 8], [1, hw]])),
                    reads=[k.WBtok["wa" if wsrc is k.WB["wa"] else "wg1"]], writes=[wsb])
            pA, pG = PBS[0 + (hp % 2)], PBS[2 + (hp % 2)]
            for fi in range(8):
                emit("T", lambda e, fi=fi, hw=hw, wav=wav, pA=pA: e.matmul(
                    pA[0:hw, 0:512], wav[:, fi * hw:(fi + 1) * hw], f_hln[fi].a(),
                    start=(fi == 0), stop=(fi == 7)),
                    reads=[wav, f_hln[fi]], writes=[pA])
            for fi in range(8):
                emit("T", lambda e, fi=fi, hw=hw, wgv=wgv, pG=pG: e.matmul(
                    pG[0:hw, 0:512], wgv[:, fi * hw:(fi + 1) * hw], f_hln[fi].a(),
                    start=(fi == 0), stop=(fi == 7)),
                    reads=[wgv, f_hln[fi]], writes=[pG])
            emit("S", lambda e, hp=hp, hw=hw, pG=pG: e.activation(
                f_gel.a(p1=hw), pG[0:hw, 0:512], AF.Gelu,
                bias=k.c_b1g[0:hw, l * NHC + hp: l * NHC + hp + 1]),
                reads=[pG, k.c_b1g], writes=[f_gel])
            emit("V", lambda e, hp=hp, hw=hw, pA=pA: e.scalar_tensor_tensor(
                f_ag[hp].a(p1=hw), pA[0:hw, 0:512],
                k.c_b1a[0:hw, l * NHC + hp: l * NHC + hp + 1], f_gel.a(p1=hw),
                ALU.add, ALU.mult),
                reads=[pA, k.c_b1a, f_gel], writes=[f_ag[hp]])

        if 'w2' in FFN_SUB:
            for hp in range(NHC):
                hw = 128 if hp < NHC - 1 else HID - 128 * (NHC - 1)
                wv2 = w2v[hp % 2]
                emit("DQ", lambda e, hp=hp, hw=hw, wv2=wv2: e.dma_start(
                    wv2.a(0, hw, 0, 1024),
                    k.WB["w2"][l][128 * hp:128 * hp + hw, 0:1024]),
                    reads=[k.WBtok["w2"]], writes=[wv2])
                for fc in range(8):
                    pw = PBS[fc]
                    emit("T", lambda e, hp=hp, hw=hw, fc=fc, wv2=wv2, pw=pw: e.matmul(
                        pw[:, 0:512],
                        wv2.a(0, hw, fc * 128, fc * 128 + 128),
                        f_ag[hp].a(p1=hw),
                        start=(hp == 0), stop=(hp == NHC - 1)),
                        reads=[wv2, f_ag[hp]], writes=[pw])
            for fc in range(8):
                pw = PBS[fc]
                emit("V", lambda e, fc=fc, tq=tq, pw=pw: e.scalar_tensor_tensor(
                    k.xf[fc][:, 1 + tq * TT: 1 + (tq + 1) * TT], pw[:, 0:512],
                    k.c_b2[:, l * 8 + fc: l * 8 + fc + 1],
                    k.xf[fc][:, 1 + tq * TT: 1 + (tq + 1) * TT], ALU.add, ALU.add),
                    reads=[pw, k.c_b2, k.xf[fc]], writes=[k.xf[fc]])


def _tap_xf_final(k, b, l):
    if b == 0 and l == 0:
        for fi in range(8):
            _tap(k, f"xf_l0_{fi}", k.xf[fi][:, 0:2052], [k.xf[fi]])


def _assemble(k, b):
    P, emit, pool, PBS = k.P, k.P.emit, k.pool, k.PBS
    pool.reset()
    tmx = [pool.take(128, D, F32) for _ in range(2)]
    outb = [pool.take(128, D, F32) for _ in range(2)]
    for tc in range(16):
        tm = tmx[tc % 2]; ob = outb[tc % 2]
        emit("DQ", lambda e, tm=tm, tc=tc: e.dma_start(
            tm.a(), k.x_in[b, tc * 128:(tc + 1) * 128, :]), writes=[tm])
        base = 4 * (tc % 2)
        pxa, pxb = PBS[base], PBS[base + 1]
        paa, pab = PBS[base + 2], PBS[base + 3]
        for fc in range(8):
            pp = pxa if fc < 4 else pxb
            co = (fc % 4) * 128
            emit("T", lambda e, fc=fc, pp=pp, co=co, tc=tc: e.transpose(
                pp[:, co:co + 128], k.xf[fc][:, 1 + tc * 128: 1 + (tc + 1) * 128],
                k.c_id32[:, :]),
                reads=[k.xf[fc], k.c_id32], writes=[pp])
            if STAGES["attn"]:
                pq = paa if fc < 4 else pab
                emit("T", lambda e, fc=fc, pq=pq, co=co, tc=tc: e.matmul(
                    pq[:, co:co + 128], k.acc[fc][:, tc * 128:(tc + 1) * 128],
                    k.c_idbf[:, :], start=True, stop=True),
                    reads=[k.acc[fc], k.c_idbf], writes=[pq])
        emit("V", lambda e, tm=tm, ob=ob, pxa=pxa: e.tensor_add(
            ob.a(c0=0, c1=512), tm.a(c0=0, c1=512), pxa[:, 0:512]),
            reads=[tm, pxa], writes=[ob])
        emit("V", lambda e, tm=tm, ob=ob, pxb=pxb: e.tensor_add(
            ob.a(c0=512, c1=1024), tm.a(c0=512, c1=1024), pxb[:, 0:512]),
            reads=[tm, pxb], writes=[ob])
        if STAGES["attn"]:
            emit("V", lambda e, ob=ob, paa=paa: e.tensor_add(
                ob.a(c0=0, c1=512), ob.a(c0=0, c1=512), paa[:, 0:512]),
                reads=[paa, ob], writes=[ob])
            emit("V", lambda e, ob=ob, pab=pab: e.tensor_add(
                ob.a(c0=512, c1=1024), ob.a(c0=512, c1=1024), pab[:, 0:512]),
                reads=[pab, ob], writes=[ob])
        emit("GQ", lambda e, ob=ob, tc=tc: e.dma_start(
            k.out_d[b, tc * 128:(tc + 1) * 128, :], ob.a()), reads=[ob],
            dram_store=True)



def _host_prep(inputs):
    tq, tqs, tk, tks = _ret_tables()
    fq = _ffn_tables()
    cn = _consts()
    w1 = np.asarray(inputs["L0_ffn_w1"])
    b1 = np.asarray(inputs["L0_ffn_b1"])
    qs_i = np.asarray(inputs["L0_ffn_q_scale"], dtype=np.float32)
    ks_i = np.asarray(inputs["L0_ffn_k_scale"], dtype=np.float32)

    def rep(v):
        return np.concatenate([v, v], axis=1)

    def sw(v):
        return np.concatenate([v[:, 32:64], v[:, 0:32]], axis=1)

    qs_all = np.stack([rep(qs_i), rep(sw(qs_i)), rep(ks_i), rep(sw(ks_i))],
                      axis=2).astype(np.float32)

    def colpad(v, nch):
        out = np.zeros((2, 128, nch), dtype=np.float32)
        kk = v.shape[1]
        for c in range(nch):
            w = min(128, kk - c * 128)
            out[:, 0:w, c] = v[:, c * 128:c * 128 + w]
        return out

    common = {
        "wq": np.ascontiguousarray(inputs["L0_attn_wq"], dtype=np.float32),
        "wk": np.ascontiguousarray(inputs["L0_attn_wk"], dtype=np.float32),
        "wv": np.ascontiguousarray(inputs["L0_attn_wv"], dtype=np.float32),
        "wg": np.ascontiguousarray(inputs["L0_attn_wg"], dtype=np.float32),
        "wo": np.ascontiguousarray(inputs["L0_attn_wo"], dtype=np.float32),
        "wqkv": np.ascontiguousarray(inputs["L0_ffn_wqkv"], dtype=np.float32),
        "wof": np.ascontiguousarray(inputs["L0_ffn_wo"], dtype=np.float32),
        "wa": np.ascontiguousarray(w1[:, :, 0:HID], dtype=np.float32),
        "wg1": np.ascontiguousarray(w1[:, :, HID:], dtype=np.float32),
        "w2": np.ascontiguousarray(inputs["L0_ffn_w2"], dtype=np.float32),
        "qs_c": qs_all,
        "lng_c": colpad(np.asarray(inputs["L0_ffn_ln_g"], dtype=np.float32), 8),
        "lnb_c": colpad(np.asarray(inputs["L0_ffn_ln_b"], dtype=np.float32), 8),
        "b1a_c": colpad(np.ascontiguousarray(b1[:, 0:HID]).astype(np.float32), NHC),
        "b1g_c": colpad(np.ascontiguousarray(b1[:, HID:]).astype(np.float32), NHC),
        "b2_c": colpad(np.asarray(inputs["L0_ffn_b2"], dtype=np.float32), 8),
        "t_tab0": tq, "t_tab1": tqs, "t_tab2": tk, "t_tab3": tks,
        "f_tab0": fq[0], "f_tab1": fq[1], "f_tab2": fq[2], "f_tab3": fq[3],
    }
    common.update(cn)
    x = np.asarray(inputs["L0_attn_x"], dtype=np.float32)
    return [dict(common, x=np.ascontiguousarray(x[2 * c:2 * c + 2])) for c in range(NCORES)]


def kernel(**inputs):
    maps = _host_prep(inputs)
    k = build_program()
    res = run_bass_kernel_spmd(k.nc, maps, core_ids=list(range(NCORES)))
    return np.concatenate([r["out"] for r in res.results], axis=0).astype(np.float32)
